# version 06
# baseline (speedup 1.0000x reference)
import math
import numpy as np
import ml_dtypes
import concourse.bass as bass
import concourse.mybir as mybir
from concourse.bass_utils import run_bass_kernel_spmd

F32 = mybir.dt.float32
BF16 = mybir.dt.bfloat16
AF = mybir.ActivationFunctionType
ALU = mybir.AluOpType
AXX = mybir.AxisListType.X

NCORES = 8
D = 4096
RPC = 1024
NSUB = RPC // 128
RT = 512
NRT = RPC // RT
SUBRT = RT // 128
ICC = D // 128
H8 = 1366
D8 = 512
KB = 8
CB = 11
HC = KB * CB
NHEAD = 32
HD = 128
SQD = 1.0 / math.sqrt(HD)
B0 = 20.0
EPS = 1e-5
NW_GU = 8 * H8 * D8


def _oct_tables():
    TRIPLES = [(1, 2, 3), (1, 4, 5), (1, 7, 6), (2, 4, 6), (2, 5, 7), (3, 4, 7), (3, 6, 5)]
    S = np.zeros((8, 8, 8), np.float32)
    S[0, 0, 0] = 1.0
    for i in range(1, 8):
        S[0, i, i] = 1.0
        S[i, 0, i] = 1.0
        S[i, i, 0] = -1.0
    for a, b, c in TRIPLES:
        for x, y, z in ((a, b, c), (b, c, a), (c, a, b)):
            S[x, y, z] = 1.0
            S[y, x, z] = -1.0
    J = [[None] * 8 for _ in range(8)]
    G = [[0.0] * 8 for _ in range(8)]
    for i in range(8):
        for k in range(8):
            js = [j for j in range(8) if S[j, i, k] != 0]
            assert len(js) == 1
            J[i][k] = js[0]
            G[i][k] = float(S[js[0], i, k])
    return J, G


OCT_J, OCT_G = _oct_tables()


class Sem:
    __slots__ = ("h", "count", "name")

    def __init__(self, nc, name):
        self.h = nc.alloc_semaphore(name)
        self.count = 0
        self.name = name


class Plan:
    ENGINES = ("sync", "scalar", "tensor", "vector", "gpsimd")

    def __init__(self, nc):
        self.nc = nc
        self.q = {e: [] for e in self.ENGINES}
        self._sems = {}

    def sem(self, name):
        if name not in self._sems:
            self._sems[name] = Sem(self.nc, name)
        return self._sems[name]

    def step(self, eng, fn, waits=(), inc=None, inc_amt=1):
        tok = None
        if inc is not None:
            inc.count += inc_amt
            tok = (inc, inc.count)
        ws = latest(*waits)
        self.q[eng].append((ws, fn, tok, inc_amt))
        return tok

    def dma(self, eng, out, in_, waits=(), sem=None, stream=None, **kw):
        if sem is None:
            assert stream is not None
            sem = self.sem("D_" + stream)
        return self.step(eng, lambda e: e.dma_start(out=out, in_=in_, **kw),
                         waits=waits, inc=sem, inc_amt=16)

    def realize(self):
        streams = self.q
        with self.nc.Block() as block:
            def mk(ename):
                def body(eng):
                    waited = {}
                    for ws, fn, tok, amt in streams[ename]:
                        for (s, v) in ws:
                            if waited.get(s.name, 0) >= v:
                                continue
                            eng.wait_ge(s.h, v)
                            waited[s.name] = v
                        ins = fn(eng) if fn is not None else None
                        if tok is not None:
                            (ins if ins is not None else eng.nop()).then_inc(tok[0].h, amt)
                return body
            block.sync(mk("sync"))
            block.scalar(mk("scalar"))
            block.tensor(mk("tensor"))
            block.vector(mk("vector"))
            block.gpsimd(mk("gpsimd"))


def latest(*toks):
    best = {}

    def add(t):
        if t is None:
            return
        if isinstance(t, tuple) and len(t) == 2 and isinstance(t[0], Sem):
            s, v = t
            if s.name not in best or best[s.name][1] < v:
                best[s.name] = (s, v)
        else:
            for x in t:
                add(x)

    for t in toks:
        add(t)
    return tuple(best.values())


def build_program(debug_taps=False):
    nc = bass.Bass("TRN2", target_bir_lowering=False, debug=False)

    x0 = nc.dram_tensor("x0", [RPC, D], F32, kind="ExternalInput").ap()
    wq = nc.dram_tensor("wq", [D, D], F32, kind="ExternalInput").ap()
    wk = nc.dram_tensor("wk", [D, D], F32, kind="ExternalInput").ap()
    wv = nc.dram_tensor("wv", [D, D], F32, kind="ExternalInput").ap()
    bq = nc.dram_tensor("bq", [D], F32, kind="ExternalInput").ap()
    bk = nc.dram_tensor("bk", [D], F32, kind="ExternalInput").ap()
    bv = nc.dram_tensor("bv", [D], F32, kind="ExternalInput").ap()
    wg = nc.dram_tensor("wg", [8, H8, D8], F32, kind="ExternalInput").ap()
    wu = nc.dram_tensor("wu", [8, H8, D8], F32, kind="ExternalInput").ap()
    wd = nc.dram_tensor("wd", [8, D8, H8], F32, kind="ExternalInput").ap()
    out = nc.dram_tensor("out", [RPC, D], F32, kind="ExternalOutput").ap()

    wqB = nc.dram_tensor("wqB", [D, D], BF16).ap()
    wkB = nc.dram_tensor("wkB", [D, D], BF16).ap()
    wvB = nc.dram_tensor("wvB", [D, D], BF16).ap()
    wqT = nc.dram_tensor("wqT", [D, D], BF16).ap()
    wkT = nc.dram_tensor("wkT", [D, D], BF16).ap()
    wvT = nc.dram_tensor("wvT", [D, D], BF16).ap()
    scr_g = nc.dram_tensor("scr_g", [CB, 128, KB, 4, 128], BF16).ap()
    scr_u = nc.dram_tensor("scr_u", [CB, 128, KB, 4, 128], BF16).ap()
    scr_d = nc.dram_tensor("scr_d", [4, 128, KB, CB, 128], BF16).ap()
    h_scr = nc.dram_tensor("h_scr", [HC, 128, RT], BF16).ap()
    fT_scr = nc.dram_tensor("fT_scr", [RT, D], BF16).ap()
    x1_scr = nc.dram_tensor("x1_scr", [RPC, D], F32).ap()
    x2_scr = nc.dram_tensor("x2_scr", [RPC, D], F32).ap()
    ctx_scr = nc.dram_tensor("ctx_scr", [RPC, D], BF16).ap()
    vt_scr = nc.dram_tensor("vt_scr", [D, RT], BF16).ap()

    identD = nc.inline_tensor(np.eye(128, dtype=ml_dtypes.bfloat16), name="identc").ap()
    ones1fD = nc.inline_tensor(np.ones((1, 128), np.float32), name="ones1fc").ap()
    ones1bD = nc.inline_tensor(np.ones((1, 128), ml_dtypes.bfloat16), name="ones1bc").ap()
    onescbD = nc.inline_tensor(np.ones((128, 1), ml_dtypes.bfloat16), name="onescbc").ap()
    onescfD = nc.inline_tensor(np.ones((128, 1), np.float32), name="onescfc").ap()
    constsD = nc.inline_tensor(
        np.tile(np.array([[0.5, -0.5, EPS, 0.0]], np.float32), (128, 1)), name="constsc").ap()

    ident = nc.alloc_sbuf_tensor("ident", [128, 128], BF16)
    ones1f = nc.alloc_sbuf_tensor("ones1f_s", [1, 128], F32)
    ones1b = nc.alloc_sbuf_tensor("ones1b_s", [1, 128], BF16)
    onescb = nc.alloc_sbuf_tensor("onescb_s", [128, 1], BF16)
    onescf = nc.alloc_sbuf_tensor("onescf_s", [128, 1], F32)
    consts = nc.alloc_sbuf_tensor("consts", [128, 4], F32)
    scales = nc.alloc_sbuf_tensor("scales", [128, 8], F32)
    C_RG, C_RU, C_RD, C_SG2, C_SU2, C_SD4, C_EB = range(7)
    sstat = nc.alloc_sbuf_tensor("sstat", [128, 8], F32)
    redq = nc.alloc_sbuf_tensor("redq", [128, 112], F32)
    sval = nc.alloc_sbuf_tensor("sval", [1, 16], F32)
    bqT = nc.alloc_sbuf_tensor("bqT", [128, ICC], F32)
    bkT = nc.alloc_sbuf_tensor("bkT", [128, ICC], F32)
    bvT = nc.alloc_sbuf_tensor("bvT", [128, ICC], F32)
    nrmio = nc.alloc_sbuf_tensor("nrmio", [1, 1], F32)

    offA, _ = nc.bump_sbuf(65536)
    offB, _ = nc.bump_sbuf(65536)
    offC, _ = nc.bump_sbuf(32768)
    offD, _ = nc.bump_sbuf(24576)
    offE, _ = nc.bump_sbuf(16384)

    def at(name, shape, dtype, base, off):
        return nc.alloc_sbuf_tensor_at(name, shape, dtype, offset=base + off)

    lnT = at("lnT", [128, ICC * RT], BF16, offA, 0)
    lnTn = at("lnTn", [128, ICC * RT], BF16, offA, 32768)
    qTr = at("qTr", [128, ICC * RT], BF16, offA, 0)
    kTr = at("kTr", [128, ICC * RT], BF16, offA, 32768)
    qsrc = [at(f"qsrc{i}", [128, 1408], F32, offB, i * 5632) for i in range(2)]
    qtq = at("qtq", [128, 1408], BF16, offB, 11264)
    qtq2 = at("qtq2", [128, 1408], BF16, offB, 14080)
    qstag = at("qstag", [128, 1408], BF16, offB, 16896)
    ln2T = at("ln2T", [128, ICC * RPC], BF16, offB, 0)
    lnT2 = at("lnT2", [128, ICC * RT], BF16, offB, 0)
    lnTn2 = at("lnTn2", [128, ICC * RT], BF16, offB, 32768)
    dsln2 = at("dsln2", [128, HC * 128], BF16, offB, 0)

    gsl = [at(f"gsl{i}", [128, 4096], BF16, offC, i * 8192) for i in range(2)]
    usl = [at(f"usl{i}", [128, 4096], BF16, offC, 16384 + i * 8192) for i in range(2)]
    dsl = at("dsl", [128, HC * 128], BF16, offC, 0)
    dsln = at("dsln", [128, HC * 128], BF16, offA, 0)
    sqs = at("sqs", [128, D], BF16, offC, 0)
    ftsub = at("ftsub", [128, D], BF16, offC, 8192)
    wst = [at(f"wst{i}", [128, ICC * 128], BF16, offC, i * 8192) for i in range(2)]
    wvcol = at("wvcol", [128, ICC * 256], BF16, offC, 16384)
    xsub = [at("xsub0", [128, D], F32, offD, 0)]
    hst = at("hst", [128, 4 * RT], BF16, offD, 0)
    silu_t = at("silu_t", [128, RT], F32, offD, 4096)
    dstag = at("dstag", [128, RT], BF16, offD, 6144)
    fstag = at("fstag", [128, 512], BF16, offD, 7168)
    hrd = [at(f"hrd{i}", [128, 8 * RT], BF16, offD, 8192 + i * 8192) for i in range(2)]

    vthalf = at("vthalf", [128, ICC * 256], BF16, offD, 0)
    cs = at("cs", [32, 32 * 128], BF16, offD, 16384)
    brow = at("brow", [1, D], F32, offE, 0)
    tstag = at("tstag", [128, D], BF16, offE, 0)
    sub_sb = at("sub_sb", [32, 512], F32, offE, 0)
    prob = at("prob", [32, 512], F32, offE, 2048)
    pn = at("pn", [32, 512], BF16, offE, 4096)
    pt = at("pt", [32, 512], BF16, offE, 5120)
    vsb = at("vsb", [32, 16 * 128], BF16, offE, 6144)
    mxs = at("mxs", [32, 16], F32, offE, 10240)
    vstg = at("vstg", [128, RT], BF16, offE, 11264)
    dens = at("dens", [32, 16], F32, offE, 10496)
    recs = at("recs", [32, 16], F32, offE, 10752)
    qscu = at("qscu", [128, 2048], F32, offE, 8192)

    pb = [nc.alloc_psum_tensor(f"pb{i}", [128, 512], F32) for i in range(8)]
    pbB = [p.ap().bitcast(BF16) for p in pb]

    plan = Plan(nc)
    sIN = plan.sem("sIN")
    sSC = plan.sem("sSC")
    sGP = plan.sem("sGP")
    sPE = plan.sem("sPE")
    sDV = plan.sem("sDV")
    sAC = plan.sem("sAC")
    sCC = plan.sem("sCC")
    sOUT = plan.sem("sOUT")

    xguard = {0: (), 1: ()}
    ftguard = {"t": ()}

    cast_toks = {}
    cast_steps = []
    for name, (srcm, dstB) in {"v": (wv, wvB), "q": (wq, wqB), "k": (wk, wkB)}.items():
        for ch in range(16):
            def go_cast(name=name, srcm=srcm, dstB=dstB, ch=ch):
                cast_toks[name] = plan.dma(
                    "gpsimd", dstB[ch * 256:(ch + 1) * 256, :],
                    srcm[ch * 256:(ch + 1) * 256, :], stream="cast_" + name)
            cast_steps.append(go_cast)

    tcl = []
    for dst, src in ((ident, identD), (ones1f, ones1fD), (ones1b, ones1bD),
                     (onescb, onescbD), (onescf, onescfD), (consts, constsD)):
        sh = dst.shape
        tcl.append(plan.dma("sync", dst[0:sh[0], 0:sh[1]], src, stream="const"))
    tok_consts = latest(*tcl)

    def bias_transpose(bsrc, bdst, extra):
        tok_l = plan.dma("sync", brow[0:1, :], bsrc[None, :], stream="brow", waits=extra)
        tok_mm = None
        for ch in range(ICC):
            tok_mm = plan.step("tensor",
                               (lambda e, ch=ch: e.matmul(pb[6][:, ch:ch + 1],
                                                          brow[0:1, ch * 128:(ch + 1) * 128],
                                                          ones1f[0:1, 0:1],
                                                          start=(ch == 0), stop=(ch == ICC - 1))),
                               waits=(tok_l, tok_consts) if ch == 0 else (),
                               inc=sPE if ch == ICC - 1 else None)
        tok_ev = plan.step("vector",
                           (lambda e: e.tensor_copy(out=bdst[0:128, 0:ICC], in_=pb[6][:, 0:ICC])),
                           waits=(tok_mm,), inc=sDV)
        return tok_ev

    tok_bqT = bias_transpose(bq, bqT, ())
    tok_bkT = bias_transpose(bk, bkT, (tok_bqT,))
    tok_bvT = bias_transpose(bv, bvT, (tok_bkT,))

    tok_ms_qstag = plan.step("vector", lambda e: e.memset(qstag[:, :], 0.0), inc=sDV)

    qsc_g = [usl[0].ap().bitcast(F32), usl[1].ap().bitcast(F32)]
    qsc_u = [qscu[:, 0:1024], qscu[:, 1024:2048]]

    def quant_scale(wsrc, nelem, col_recip, col_out, out_div, queue, bufs, bufw,
                    guards, rqbase, use_act):
        flat = wsrc.rearrange("a b c -> (a b c)").rearrange("(p n) -> p n", p=128)
        percol = nelem // 128
        tok_r = None
        ci = 0
        pos = 0
        while pos < percol:
            cw = min(bufw, percol - pos)
            sl = ci % 2
            tok_l = plan.dma(queue, bufs[sl][:, 0:cw], flat[:, pos:pos + cw],
                             stream=f"qsc{queue}{sl}", waits=tuple(guards[sl]))
            if use_act:
                tok_r = plan.step("scalar",
                                  (lambda e, sl=sl, cw=cw, ci=ci:
                                   e.activation(out=sqs[:, 0:cw].bitcast(F32), in_=bufs[sl][:, 0:cw],
                                                func=AF.Abs,
                                                accum_out=redq[:, rqbase + ci:rqbase + ci + 1])),
                                  waits=(tok_l,), inc=sAC)
            else:
                tok_r = plan.step("vector",
                                  (lambda e, sl=sl, cw=cw, ci=ci:
                                   e.tensor_reduce(out=redq[:, rqbase + ci:rqbase + ci + 1],
                                                   in_=bufs[sl][:, 0:cw],
                                                   axis=AXX, op=ALU.add, apply_absolute_value=True)),
                                  waits=(tok_l,), inc=sDV)
            guards[sl] = (tok_r,)
            pos += cw
            ci += 1
        tok_r2 = plan.step("vector",
                           (lambda e, ci=ci:
                            e.tensor_reduce(out=redq[:, rqbase + ci:rqbase + ci + 1],
                                            in_=redq[:, rqbase:rqbase + ci],
                                            axis=AXX, op=ALU.add)),
                           waits=tuple(guards[0]) + tuple(guards[1]), inc=sDV)
        tok_mm = plan.step("tensor",
                           (lambda e, ci=ci: e.matmul(pb[6][0:1, 256:257],
                                                      redq[:, rqbase + ci:rqbase + ci + 1],
                                                      onescf[:, 0:1],
                                               start=True, stop=True)),
                           waits=(tok_r2,), inc=sPE)
        tok_s = plan.step("vector",
                          lambda e: e.tensor_copy(out=sval[0:1, 0:1], in_=pb[6][0:1, 256:257]),
                          waits=(tok_mm,), inc=sDV)
        tok_rc = plan.step("vector", lambda e: e.reciprocal(out=sval[0:1, 1:2], in_=sval[0:1, 0:1]),
                           waits=(tok_s,), inc=sDV)
        tok_rs = plan.step("vector",
                           (lambda e: e.tensor_scalar(out=sval[0:1, 2:3], in0=sval[0:1, 1:2],
                                                      scalar1=float(nelem), scalar2=None, op0=ALU.mult)),
                           waits=(tok_rc,), inc=sDV)
        tok_os = plan.step("vector",
                           (lambda e: e.tensor_scalar(out=sval[0:1, 3:4], in0=sval[0:1, 0:1],
                                                      scalar1=float(1.0 / (nelem * out_div)),
                                                      scalar2=None, op0=ALU.mult)),
                           waits=(tok_s,), inc=sDV)
        tok_b1 = plan.step("tensor",
                           (lambda e: e.matmul(pb[6][:, 257:258], ones1f[0:1, :], sval[0:1, 2:3],
                                               start=True, stop=False)),
                           waits=(tok_rs, tok_os), inc=None)
        tok_b2 = plan.step("tensor",
                           (lambda e: e.matmul(pb[6][:, 258:259], ones1f[0:1, :], sval[0:1, 3:4],
                                               start=False, stop=True)),
                           waits=(), inc=sPE)
        tok_e1 = plan.step("vector",
                           (lambda e, col_recip=col_recip:
                            e.tensor_copy(out=scales[:, col_recip:col_recip + 1],
                                          in_=pb[6][:, 257:258])),
                           waits=(tok_b2,), inc=sDV)
        tok_e2 = plan.step("vector",
                           (lambda e, col_out=col_out:
                            e.tensor_copy(out=scales[:, col_out:col_out + 1],
                                          in_=pb[6][:, 258:259])),
                           waits=(tok_b2,), inc=sDV)
        return latest(tok_e1, tok_e2)

    pchain = {"pb6": None, "pb7": None, "pb4": None, "pb5": None,
              "pb2": None, "pb3": None}
    qchain = {"mm": None, "w": None}
    qslot = [(), ()]

    def quant_ternary(src_ap, rows, cols, col_recip, wts):
        tok_s1 = plan.step("scalar",
                           (lambda e: e.activation(out=qtq[0:rows, 0:cols], in_=src_ap,
                                                   func=AF.Sign, bias=consts[0:rows, 0:1],
                                                   scale=scales[0:rows, col_recip:col_recip + 1])),
                           waits=wts, inc=sAC)
        tok_s2 = plan.step("scalar",
                           (lambda e: e.activation(out=qtq2[0:rows, 0:cols], in_=src_ap,
                                                   func=AF.Sign, bias=consts[0:rows, 1:2],
                                                   scale=scales[0:rows, col_recip:col_recip + 1])),
                           waits=(tok_s1,), inc=sAC)
        tok_t = plan.step("vector",
                          (lambda e: e.tensor_tensor(out=qtq[0:rows, 0:cols],
                                                     in0=qtq[0:rows, 0:cols],
                                                     in1=qtq2[0:rows, 0:cols], op=ALU.add)),
                          waits=(tok_s2,), inc=sDV)
        return tok_t

    qctr = {"n": 0}
    scr_g_done = {}
    scr_u_done = {}
    scr_d_done = {}

    def quant_gu_step(wsrc, scr, col_recip, tok_scale, oct_, j, done):
        rows = 128 if oct_ < 10 else H8 - 1280
        sl = qctr["n"] % 2
        qctr["n"] += 1
        tok_l = plan.dma("sync", qsrc[sl][0:rows, 0:D8],
                         wsrc[j, oct_ * 128:oct_ * 128 + rows, :],
                         stream=f"qs{sl}", waits=qslot[sl])
        tok_t = quant_ternary(qsrc[sl][0:rows, 0:D8], rows, D8, col_recip,
                              (tok_l, tok_scale, tok_ms_qstag, qchain["mm"]))
        qslot[sl] = (tok_t,)
        tok_mm = None
        for icc4 in range(4):
            tok_mm = plan.step("tensor",
                               (lambda e, icc4=icc4, rows=rows:
                                e.matmul(pbB[7][:, icc4 * 128:icc4 * 128 + rows],
                                         qtq[0:rows, icc4 * 128:(icc4 + 1) * 128],
                                         ident[0:rows, 0:rows],
                                         start=(icc4 == 0), stop=(icc4 == 3),
                                         is_transpose=True)),
                               waits=latest(tok_t, pchain["pb7"]) if icc4 == 0 else (),
                               inc=sPE if icc4 == 3 else None)
        qchain["mm"] = tok_mm
        tok_pad = None
        if rows < 128:
            tok_pad = plan.step("vector",
                                (lambda e, rows=rows:
                                 e.memset(qstag[:, 0:512]
                                          .rearrange("p (i c) -> p i c", i=4)[:, :, rows:128],
                                          0.0)),
                                waits=latest(qchain["w"],), inc=sDV)
        tok_ev = plan.step("vector",
                           (lambda e, rows=rows:
                            e.tensor_copy(
                                out=qstag[:, 0:512].rearrange("p (i c) -> p i c", i=4)[:, :, 0:rows],
                                in_=pbB[7][:, 0:512].rearrange("p (i c) -> p i c", i=4)[:, :, 0:rows])),
                           waits=latest(tok_mm, qchain["w"], tok_pad), inc=sDV)
        pchain["pb7"] = tok_ev
        tok_w = plan.dma("scalar", scr[oct_, :, j, :, :],
                         qstag[:, 0:512].rearrange("p (i c) -> p i c", i=4),
                         stream="wscr", waits=latest(tok_ev, tok_pad))
        qchain["w"] = tok_w
        if j == 7:
            done[oct_] = tok_w

    def quant_d_step(tok_scale, w, j):
        sl = qctr["n"] % 2
        qctr["n"] += 1
        tok_l = plan.dma("sync", qsrc[sl][:, 0:H8], wd[j, w * 128:(w + 1) * 128, :],
                         stream=f"qs{sl}", waits=qslot[sl])
        if True:
            if True:
                tok_t = quant_ternary(qsrc[sl][:, 0:H8], 128, H8, C_RD,
                                      (tok_l, tok_scale, tok_ms_qstag, qchain["mm"]))
                qslot[sl] = (tok_t,)
                tok7 = tok6 = None
                for dc in list(range(8)) + [8, 10, 9]:
                    dw = 128 if dc < 10 else H8 - 1280
                    if dc < 8:
                        bank, pos = pbB[7], dc
                        first, lastc = dc == 0, dc == 7
                        chain = pchain["pb7"]
                    else:
                        bank, pos = pbB[6], dc - 8
                        first, lastc = dc == 8, dc == 9
                        chain = pchain["pb6"]
                    wts = latest(tok_t, chain) if first else ()
                    tk = plan.step("tensor",
                                   (lambda e, dc=dc, dw=dw, bank=bank, pos=pos,
                                           first=first, lastc=lastc:
                                    e.matmul(bank[0:dw, pos * 128:(pos + 1) * 128],
                                             qtq[:, dc * 128:dc * 128 + dw],
                                             ident[:, :],
                                             start=first, stop=lastc, is_transpose=True)),
                                   waits=wts, inc=sPE if lastc else None)
                    if dc == 7:
                        tok7 = tk
                    if dc == 9:
                        tok6 = tk
                qchain["mm"] = latest(tok7, tok6)
                tok_e1 = plan.step("vector",
                                   lambda e: e.tensor_copy(out=qstag[:, 0:1024], in_=pbB[7][:, 0:1024]),
                                   waits=latest(tok7, qchain["w"]), inc=sDV)
                tok_e2 = plan.step("vector",
                                   lambda e: e.tensor_copy(out=qstag[:, 1024:1280], in_=pbB[6][:, 0:256]),
                                   waits=(tok6,), inc=sDV)
                tok_e3 = plan.step("vector",
                                   (lambda e: e.tensor_copy(out=qstag[0:H8 - 1280, 1280:1408],
                                                            in_=pbB[6][0:H8 - 1280, 256:384])),
                                   waits=(tok6,), inc=sDV)
                pchain["pb7"] = tok_e1
                pchain["pb6"] = tok_e3
                tok_w = plan.dma("scalar", scr_d[w, :, j, :, :],
                                 qstag[:, 0:CB * 128].rearrange("p (i c) -> p i c", i=CB),
                                 stream="wscr", waits=(tok_e1, tok_e2, tok_e3))
                qchain["w"] = tok_w
        if j == 7:
            scr_d_done[w] = tok_w

    quant_steps = []
    Q_TOTAL = 16 * CB + 32

    def emit_scales_and_steps():
        tok_sg = quant_scale(wg, NW_GU, C_RG, C_SG2, 2.0, "sync", qsc_g, 2048,
                             [(), ()], 0, False)
        tok_su = quant_scale(wu, NW_GU, C_RU, C_SU2, 2.0, "scalar", qsc_u, 1024,
                             [(), ()], 24, False)
        tok_sd = quant_scale(wd, NW_GU, C_RD, C_SD4, 4.0, "gpsimd", qsrc, 1408,
                             qslot, 70, False)
        for c_ in range(CB):
            for j_ in range(8):
                quant_steps.append(lambda c_=c_, j_=j_, ts=tok_sg:
                                   quant_gu_step(wg, scr_g, C_RG, ts, c_, j_, scr_g_done))
            for j_ in range(8):
                quant_steps.append(lambda c_=c_, j_=j_, ts=tok_su:
                                   quant_gu_step(wu, scr_u, C_RU, ts, c_, j_, scr_u_done))
        for w_ in range(4):
            for j_ in range(8):
                quant_steps.append(lambda w_=w_, j_=j_, ts=tok_sd: quant_d_step(ts, w_, j_))
        assert len(quant_steps) == Q_TOTAL

    def pump_quant(n):
        for _ in range(n):
            if quant_steps:
                quant_steps.pop(0)()

    def pump_quant_to(n):
        while quant_steps and (Q_TOTAL - len(quant_steps)) < n:
            quant_steps.pop(0)()

    prep_steps = list(cast_steps)
    prep_state = {"tstag_free": (tok_bvT,), "last": None}

    for (mB, mT, ckey) in ((wvB, wvT, "v"), (wqB, wqT, "q"), (wkB, wkT, "k")):
        for icc in range(ICC):
            def go(mB=mB, mT=mT, ckey=ckey, icc=icc):
                t1 = plan.dma("scalar", tstag[:, :], mB[:, icc * 128:(icc + 1) * 128],
                              stream="prin", waits=(cast_toks[ckey],) + tuple(prep_state["tstag_free"]),
                              transpose=True)
                t2 = plan.dma("scalar", mT[icc * 128:(icc + 1) * 128, :], tstag[:, :],
                              stream="prout", waits=(t1,))
                prep_state["tstag_free"] = (t2,)
                prep_state["last"] = t2
                prep_state["done_" + ckey] = t2
            prep_steps.append(go)

    def pump_prep(n):
        for _ in range(n):
            if prep_steps:
                prep_steps.pop(0)()

    lnpass = {"evac4": None, "evac5": None, "sqs_guard": None}

    def emit_ln_pass(rt, srcs, spill_to, T_dst, T_rowbase, T_rowspan, extra_w=()):
        tok_T_all = []
        tok_spill = []
        for sub in range(SUBRT):
            s = rt * SUBRT + sub
            sl = 0
            q0, ap0 = srcs[0]
            tok_l0 = plan.dma(q0, xsub[sl][:, :], ap0[s * 128:(s + 1) * 128, :],
                              stream="xs", waits=tuple(xguard[sl]) + tuple(extra_w))
            xguard[sl] = ()
            tok_add = tok_l0
            if len(srcs) > 1:
                q1, ap1 = srcs[1]
                assert ap1.dtype == BF16
                dst1, g1 = ftsub, None
                gw = tuple(ftguard["t"])
                r1 = sub if ap1.shape[0] == RT else s
                tok_l1 = plan.dma(q1, dst1[:, :], ap1[r1 * 128:(r1 + 1) * 128, :],
                                  stream="fts", waits=gw + tuple(extra_w))
                tok_add = plan.step("vector",
                                    (lambda e, sl=sl, dst1=dst1:
                                     e.tensor_tensor(out=xsub[sl][:, :], in0=xsub[sl][:, :],
                                                     in1=dst1[:, :], op=ALU.add)),
                                    waits=(tok_l0, tok_l1), inc=sDV)
                if g1 is not None:
                    xguard[g1] = (tok_add,)
                else:
                    ftguard["t"] = (tok_add,)
            if spill_to is not None:
                qs, aps, spill_stream = spill_to
                tok_sp = plan.dma(qs, aps[s * 128:(s + 1) * 128, :], xsub[sl][:, :],
                                  stream=spill_stream, waits=(tok_add,))
                tok_spill.append(tok_sp)
            tok_sum = plan.step("vector",
                                (lambda e, sl=sl: e.reduce_sum(out=sstat[:, 0:1],
                                                               in_=xsub[sl][:, :], axis=AXX)),
                                waits=(tok_add,), inc=sDV)
            tok_sq = plan.step("scalar",
                               (lambda e, sl=sl: e.activation(out=sqs[:, :], in_=xsub[sl][:, :],
                                                              func=AF.Square,
                                                              accum_out=sstat[:, 1:2])),
                               waits=latest(tok_add, lnpass["sqs_guard"]), inc=sAC)
            tok_m = plan.step("vector",
                              (lambda e: e.tensor_scalar(out=sstat[:, 2:3], in0=sstat[:, 0:1],
                                                         scalar1=float(-1.0 / D), scalar2=None,
                                                         op0=ALU.mult)),
                              waits=(tok_sum,), inc=sDV)
            tok_m2 = plan.step("scalar",
                               (lambda e: e.activation(out=sstat[:, 3:4], in_=sstat[:, 2:3],
                                                       func=AF.Square)),
                               waits=(tok_m,), inc=sAC)
            tok_v = plan.step("vector",
                              (lambda e: e.scalar_tensor_tensor(out=sstat[:, 4:5], in0=sstat[:, 1:2],
                                                                scalar=float(1.0 / D),
                                                                in1=sstat[:, 3:4],
                                                                op0=ALU.mult, op1=ALU.subtract)),
                              waits=(tok_sq, tok_m2), inc=sDV)
            tok_lnv = plan.step("scalar",
                                (lambda e: e.activation(out=sstat[:, 5:6], in_=sstat[:, 4:5],
                                                        func=AF.Ln, bias=consts[:, 2:3])),
                                waits=(tok_v,), inc=sAC)
            tok_rstd = plan.step("scalar",
                                 (lambda e: e.activation(out=sstat[:, 6:7], in_=sstat[:, 5:6],
                                                         func=AF.Exp, scale=-0.5)),
                                 waits=(tok_lnv,), inc=sAC)
            tok_ap = plan.step("vector",
                               (lambda e, sl=sl: e.tensor_scalar(out=sqs[:, :], in0=xsub[sl][:, :],
                                                                 scalar1=sstat[:, 2:3],
                                                                 scalar2=sstat[:, 6:7],
                                                                 op0=ALU.add, op1=ALU.mult)),
                               waits=(tok_rstd, tok_sq), inc=sDV)
            xguard[sl] = xguard[sl] + (tok_ap,) + tuple(tok_spill[-1:] if spill_to else ())
            for pk in range(4):
                bank = 4 + (pk % 2)
                tok_mm = None
                for i8 in range(8):
                    icc = pk * 8 + i8
                    wts = latest(tok_ap, lnpass[f"evac{bank}"], pchain[f"pb{bank}"]) if i8 == 0 else ()
                    tok_mm = plan.step("tensor",
                                       (lambda e, bank=bank, i8=i8, icc=icc:
                                        e.matmul(pbB[bank][:, i8 * 128:(i8 + 1) * 128],
                                                 sqs[:, icc * 128:(icc + 1) * 128],
                                                 ident[:, :], start=(i8 == 0), stop=(i8 == 7),
                                                 is_transpose=True)),
                                       waits=wts, inc=sPE if i8 == 7 else None)
                tok_ev = plan.step("vector",
                                   (lambda e, bank=bank, pk=pk, sub=sub:
                                    e.tensor_copy(
                                        out=T_dst[:, :].rearrange("p (i r) -> p i r", r=T_rowspan)
                                        [:, pk * 8:(pk + 1) * 8,
                                         T_rowbase + sub * 128:T_rowbase + (sub + 1) * 128],
                                        in_=pbB[bank][:, 0:1024].rearrange("p (i r) -> p i r", r=128))),
                                   waits=(tok_mm,), inc=sDV)
                lnpass[f"evac{bank}"] = tok_ev
                pchain[f"pb{bank}"] = tok_ev
                tok_T_all.append(tok_ev)
            lnpass["sqs_guard"] = latest(*tok_T_all[-4:])
        return latest(*tok_T_all), (latest(*tok_spill) if tok_spill else None)

    def emit_ffn(tok_lnT, tok_lnTn, first_w, first_wu=(), first_wd=(), dreg_w=(),
                 lnT_t=None, lnTn_t=None, dsln_t=None, qp=False):
        lnT_t = lnT_t if lnT_t is not None else lnT
        lnTn_t = lnTn_t if lnTn_t is not None else lnTn
        dsln_t = dsln_t if dsln_t is not None else dsln
        """first_w: extra waits for the first weight-slice loads of this call.
        qp: pump the quant queue alongside this call (FFN1 rowtile 0)."""
        evac_toks = {0: None, 1: None, 2: None, 3: None}
        gu_slot_free = [(), ()]
        hst_free = ()
        hdma_toks = []
        half_toks = []
        prev_tok_h = None
        for c in range(CB):
            if qp:
                pump_quant_to(16 * (c + 1))
            sl = c % 2
            lw = tuple(gu_slot_free[sl])
            if c < 2:
                lw = lw + tuple(first_w) + (tok_lnT,)
            if qp:
                lw = lw + (scr_g_done[c],)
            tg = plan.dma("sync", gsl[sl][:, :],
                          scr_g[c].rearrange("p a b c -> p (a b c)"), stream=f"gsl{sl}", waits=lw)
            tu = plan.dma("sync", usl[sl][:, :],
                          scr_u[c].rearrange("p a b c -> p (a b c)"), stream=f"usl{sl}",
                          waits=tuple(gu_slot_free[sl]) + (tuple(first_wu) if c < 2 else ())
                          + ((scr_u_done[c],) if qp else ()))
            tok_load = latest(tg, tu)
            last_gmm = last_umm = None
            for k in range(8):
                gb = k % 2
                ub = 2 + (k % 2)
                tok_gmm = None
                for i in range(8):
                    j = OCT_J[i][k]
                    rhs_t = lnT_t if OCT_G[i][k] > 0 else lnTn_t
                    for icc4 in range(4):
                        gic = i * 4 + icc4
                        wts = ()
                        if i == 0 and icc4 == 0:
                            wts = latest(tok_load, tok_lnT, tok_lnTn, evac_toks[gb])
                        tok_gmm = plan.step("tensor",
                                            (lambda e, gb=gb, j=j, icc4=icc4, gic=gic, rhs_t=rhs_t,
                                                    sl=sl, st=(i == 0 and icc4 == 0),
                                                    sp=(i == 7 and icc4 == 3):
                                             e.matmul(pb[gb][:, :],
                                                      gsl[sl][:, (j * 4 + icc4) * 128:(j * 4 + icc4 + 1) * 128],
                                                      rhs_t[:, gic * RT:(gic + 1) * RT],
                                                      start=st, stop=sp)),
                                            waits=wts, inc=sPE if (i == 7 and icc4 == 3) else None)
                tok_umm = None
                for i in range(8):
                    j = OCT_J[i][k]
                    rhs_t = lnT_t if OCT_G[i][k] > 0 else lnTn_t
                    for icc4 in range(4):
                        gic = i * 4 + icc4
                        wts = ()
                        if i == 0 and icc4 == 0:
                            wts = latest(evac_toks[ub])
                        tok_umm = plan.step("tensor",
                                            (lambda e, ub=ub, j=j, icc4=icc4, gic=gic, rhs_t=rhs_t,
                                                    sl=sl, st=(i == 0 and icc4 == 0),
                                                    sp=(i == 7 and icc4 == 3):
                                             e.matmul(pb[ub][:, :],
                                                      usl[sl][:, (j * 4 + icc4) * 128:(j * 4 + icc4 + 1) * 128],
                                                      rhs_t[:, gic * RT:(gic + 1) * RT],
                                                      start=st, stop=sp)),
                                            waits=wts, inc=sPE if (i == 7 and icc4 == 3) else None)
                last_gmm, last_umm = tok_gmm, tok_umm
                tok_si = plan.step("scalar",
                                   (lambda e, gb=gb: e.activation(out=silu_t[:, :], in_=pb[gb][:, :],
                                                                  func=AF.Silu,
                                                                  scale=scales[:, C_SG2:C_SG2 + 1])),
                                   waits=latest(tok_gmm, *hst_free, prev_tok_h,
                                                *(dreg_w if (c == 0 and k == 0) else ())),
                                   inc=sAC)
                evac_toks[gb] = tok_si
                kk = k % 4
                tok_h = plan.step("vector",
                                  (lambda e, ub=ub, kk=kk:
                                   e.scalar_tensor_tensor(out=hst[:, kk * RT:(kk + 1) * RT],
                                                          in0=pb[ub][:, :],
                                                          scalar=scales[:, C_SU2:C_SU2 + 1],
                                                          in1=silu_t[:, :],
                                                          op0=ALU.mult, op1=ALU.mult)),
                                  waits=latest(tok_umm, tok_si), inc=sDV)
                evac_toks[ub] = tok_h
                prev_tok_h = tok_h
                half_toks.append(tok_h)
                if k == 3 or k == 7:
                    bk0 = k - 3
                    dst = h_scr.rearrange("(k c) p r -> c k p r", c=CB)[c, bk0:bk0 + 4] \
                               .rearrange("k p r -> p k r")
                    tok_hd = plan.dma("gpsimd", dst,
                                      hst[:, :].rearrange("p (g r) -> p g r", g=4),
                                      stream="hw", waits=tuple(half_toks[-4:]))
                    hdma_toks.append(tok_hd)
                    hst_free = (tok_hd,)
                if qp:
                    pump_quant(3)
            gu_slot_free[sl] = (last_gmm, last_umm)
            pump_prep(1 if qp else 6)
        tok_h_all = latest(*hdma_toks)

        fT_toks = []
        dsl_free = ()
        for w in range(4):
            if qp:
                pump_quant_to(16 * CB + 8 * (w + 1))
            lw = tuple(dsl_free) + (tok_h_all,)
            td = plan.dma("sync", dsl[:, :], scr_d[w].rearrange("p a b c -> p (a b c)"),
                          stream="dsl", waits=lw + (tuple(first_wd) if w == 0 else ())
                          + ((scr_d_done[w],) if qp else ()))
            tn = plan.step("vector",
                           (lambda e, dsln_t=dsln_t: e.tensor_scalar(out=dsln_t[:, :], in0=dsl[:, :],
                                                                     scalar1=-1.0, scalar2=None,
                                                                     op0=ALU.mult)),
                           waits=(td,), inc=sDV)
            hload = [None, None]
            hfree = [(), ()]
            last_mm = None
            for g in range(CB):
                sl = g % 2
                if g < 2:
                    hload[sl] = plan.dma("sync", hrd[sl][:, :],
                                         h_scr[g * 8:(g + 1) * 8].rearrange("g p r -> p g r"),
                                         stream=f"hrd{sl}", waits=(tok_h_all,) + tuple(hfree[sl]))
                tok_g = None
                for k2 in range(8):
                    for hci in range(8):
                        hc = g * 8 + hci
                        i2, dc = divmod(hc, CB)
                        j = OCT_J[i2][k2]
                        src = dsl if OCT_G[i2][k2] > 0 else dsln_t
                        wts = ()
                        if k2 == 0 and hci == 0:
                            wts = latest(td, tn, hload[sl],
                                         *(fT_toks[-8:] if (w > 0 and g == 0) else ()))
                        tok_g = plan.step("tensor",
                                          (lambda e, k2=k2, j=j, dc=dc, src=src, hci=hci, sl=sl,
                                                  st=(g == 0 and hci == 0),
                                                  sp=(g == CB - 1 and hci == 7):
                                           e.matmul(pb[k2][:, :],
                                                    src[:, (j * CB + dc) * 128:(j * CB + dc + 1) * 128],
                                                    hrd[sl][:, hci * RT:(hci + 1) * RT],
                                                    start=st, stop=sp)),
                                          waits=wts, inc=sPE if (k2 == 7 and hci == 7) else None)
                last_mm = tok_g
                hfree[sl] = (tok_g,)
                if g + 2 <= CB - 1:
                    hload[sl] = plan.dma("sync", hrd[sl][:, :],
                                         h_scr[(g + 2) * 8:(g + 3) * 8].rearrange("g p r -> p g r"),
                                         stream=f"hrd{sl}", waits=tuple(hfree[sl]))
                if qp:
                    pump_quant(1)
            prev_ev2 = None
            for k2 in range(8):
                tok_e = plan.step("vector",
                                  (lambda e, k2=k2:
                                   e.tensor_scalar(out=dstag[:, :], in0=pb[k2][:, :],
                                                   scalar1=scales[:, C_SD4:C_SD4 + 1], scalar2=None,
                                                   op0=ALU.mult)),
                                  waits=latest(last_mm, *(fT_toks[-1:] if fT_toks else ())),
                                  inc=sDV)
                tok_tm = None
                for t4 in range(4):
                    tok_tm = plan.step("tensor",
                                       (lambda e, k2=k2, t4=t4:
                                        e.matmul(pbB[k2][:, t4 * 128:(t4 + 1) * 128],
                                                 dstag[:, t4 * 128:(t4 + 1) * 128],
                                                 ident[:, :], start=(t4 == 0), stop=(t4 == 3),
                                                 is_transpose=True)),
                                       waits=(tok_e,) if t4 == 0 else (),
                                       inc=sPE if t4 == 3 else None)
                tok_e2 = plan.step("vector",
                                   (lambda e, k2=k2: e.tensor_copy(out=fstag[:, :],
                                                                   in_=pbB[k2][:, 0:512])),
                                   waits=latest(tok_tm, prev_ev2), inc=sDV)
                prev_ev2 = tok_e2
                dmpos = k2 * 512 + w * 128
                tok_f = plan.dma("gpsimd",
                                 fT_scr.rearrange("(t p) d -> p t d", p=128)[:, :, dmpos:dmpos + 128],
                                 fstag[:, :].rearrange("p (t c) -> p t c", t=4),
                                 stream="ftw", waits=(tok_e2,))
                fT_toks.append(tok_f)
            dsl_free = (last_mm,)
            pump_prep(1 if qp else 4)
        if qp:
            pump_quant(Q_TOTAL)
        return latest(*fT_toks)

    tok_ln2_parts = []
    tok_x1_parts = []
    for rt in range(NRT):
        tok_T, _ = emit_ln_pass(rt, [("sync", x0)], None, lnT, 0, RT)
        if rt == 0:
            emit_scales_and_steps()
        tok_n = plan.step("vector",
                          (lambda e: e.tensor_scalar(out=lnTn[:, :], in0=lnT[:, :],
                                                     scalar1=-1.0, scalar2=None, op0=ALU.mult)),
                          waits=(tok_T,), inc=sDV)
        pump_prep(4)
        tok_fT = emit_ffn(tok_T, tok_n, (), qp=(rt == 0))
        tok_ln2, tok_x1 = emit_ln_pass(rt, [("sync", x0), ("scalar", fT_scr)],
                                       ("gpsimd", x1_scr, "x1w"), ln2T, rt * RT, RPC,
                                       extra_w=(tok_fT,) + ((scr_d_done[3],)
                                                            if rt == 0 else ()))
        tok_ln2_parts.append(tok_ln2)
        tok_x1_parts.append(tok_x1)
        pump_prep(4)
    tok_ln2_all = latest(*tok_ln2_parts)
    tok_x1_all = latest(*tok_x1_parts)

    pump_prep(len(prep_steps))
    tok_prep = prep_state["last"]

    wst_free = [(), ()]
    ctx_toks = []
    cs_guard = ()
    vth_guard = ()
    mix_end_guard = ()
    last_vth = {"t": None}
    pq_evg = {0: None, 1: None}
    pchain.setdefault("pb2", None)
    pchain.setdefault("pb3", None)
    pchain.setdefault("pb4", None)
    pchain.setdefault("pb5", None)

    for rt in range(NRT):
        vdma_toks = []
        qk_ev_toks = []
        vstg_guard = ()
        for m_i, (mT, bT_t, kind) in enumerate(((wvT, bvT, "v"), (wqT, bqT, "q"),
                                                (wkT, bkT, "k"))):
            for occ in range(ICC):
                wsl = occ % 2
                lw = tuple(wst_free[wsl])
                if rt == 0 and occ < 2:
                    lw = lw + (prep_state["done_" + kind],)
                    if m_i == 0:
                        lw = lw + (tok_ln2_all,)
                if kind == "q" and occ < 2:
                    lw = lw + tuple(mix_end_guard)
                tok_wl = plan.dma("sync", wst[wsl][:, :],
                                  mT.rearrange("(i p) o -> p i o", p=128)
                                  [:, :, occ * 128:(occ + 1) * 128],
                                  stream=f"wst{wsl}", waits=lw)
                bank = occ % 2
                tok_mm = None
                for icc in range(ICC):
                    wts = ()
                    if icc == 0:
                        wts = latest(tok_wl, tok_ln2_all, tok_bvT, pq_evg[bank])
                    tok_mm = plan.step("tensor",
                                       (lambda e, icc=icc, bank=bank, wsl=wsl, rt=rt:
                                        e.matmul(pb[bank][:, :],
                                                 wst[wsl][:, icc * 128:(icc + 1) * 128],
                                                 ln2T[:, :].rearrange("p (i r) -> p i r", r=RPC)
                                                 [:, icc, rt * RT:(rt + 1) * RT],
                                                 start=(icc == 0), stop=(icc == ICC - 1))),
                                       waits=wts, inc=sPE if icc == ICC - 1 else None)
                if kind == "q":
                    dstw = qTr[:, :].rearrange("p (o r) -> p o r", r=RT)[:, occ, :]
                elif kind == "k":
                    dstw = kTr[:, :].rearrange("p (o r) -> p o r", r=RT)[:, occ, :]
                else:
                    dstw = vstg[:, :]
                ev_w = (tok_mm,) + (vstg_guard if kind == "v" else ())
                if kind == "v" and occ == 0:
                    ev_w = ev_w + (tok_x1_all,)
                tok_ev = plan.step("vector",
                                   (lambda e, bank=bank, occ=occ, dstw=dstw, bT_t=bT_t:
                                    e.tensor_scalar(out=dstw, in0=pb[bank][:, :],
                                                    scalar1=bT_t[:, occ:occ + 1], scalar2=None,
                                                    op0=ALU.add)),
                                   waits=ev_w, inc=sDV)
                pq_evg[bank] = tok_ev
                if kind == "v":
                    tok_vd = plan.dma("gpsimd", vt_scr[occ * 128:(occ + 1) * 128, :],
                                      vstg[:, :], stream="vtw",
                                      waits=latest(tok_ev, *( (last_vth["t"],) if occ == 0 else () )))
                    vdma_toks.append(tok_vd)
                    vstg_guard = (tok_vd,)
                else:
                    qk_ev_toks.append(tok_ev)
                wst_free[wsl] = (tok_mm, tok_ev)
        tok_qk_all = latest(*qk_ev_toks)

        sc_war = {4: None, 5: None}
        vsb_war = {0: None, 1: None}
        cse_chain = {0: None, 1: None, 2: None, 3: None}
        prev_exp = None
        prev_pn = None
        prev_mm2 = None
        for half in range(2):
            tok_vh = plan.dma("sync",
                              vthalf[:, :].rearrange("p (o r) -> p o r", r=256),
                              vt_scr.rearrange("(o p) r -> p o r", p=128)
                              [:, :, half * 256:(half + 1) * 256],
                              stream="vth",
                              waits=latest(*vdma_toks, *vth_guard))
            for g in range(16):
                s0 = half * 256 + g * 16
                scb = 4 + g % 2
                vtb = 6 + g % 2
                tok_mm1 = None
                for i in range(16):
                    s_ = s0 + i
                    wts = ()
                    if i == 0:
                        wts = latest(tok_qk_all, sc_war[scb], pq_evg[0], pq_evg[1],
                                     pchain[f"pb{scb}"])
                    tok_mm1 = plan.step("tensor",
                                        (lambda e, i=i, s_=s_, scb=scb:
                                         e.matmul(pb[scb][0:32, i * 32:(i + 1) * 32],
                                                  qTr[:, :].rearrange("p (h s) -> p h s", s=RT)
                                                  [:, :, s_],
                                                  kTr[:, :].rearrange("p (h s) -> p h s", s=RT)
                                                  [:, :, s_],
                                                  start=(i == 0), stop=(i == 15))),
                                        waits=wts, inc=sPE if i == 15 else None)
                tok_vt_f = []
                tok_vsb0 = None
                for fill in range(2):
                    tk = None
                    for i8 in range(8):
                        i = fill * 8 + i8
                        s_ = s0 + i
                        wts = ()
                        if i == 0:
                            wts = latest(tok_vh, vsb_war[vtb - 6], pchain[f"pb{vtb}"])
                        elif i == 8:
                            wts = latest(tok_vsb0,)
                        tk = plan.step("tensor",
                                       (lambda e, i8=i8, s_=s_, vtb=vtb, half=half:
                                        e.matmul(pbB[vtb][0:32, i8 * 128:(i8 + 1) * 128],
                                                 vthalf[:, :].rearrange("p (h s) -> p h s", s=256)
                                                 [:, :, s_ - half * 256],
                                                 ident[:, :],
                                                 start=(i8 == 0), stop=(i8 == 7),
                                                 is_transpose=True)),
                                       waits=wts, inc=sPE if i8 == 7 else None)
                    tok_vt_f.append(tk)
                    if fill == 0:
                        tok_vsb0 = plan.step("vector",
                                             (lambda e, vtb=vtb:
                                              e.tensor_copy(out=vsb[0:32, 0:1024],
                                                            in_=pbB[vtb][0:32, 0:1024])),
                                             waits=latest(tk, prev_mm2), inc=sDV)
                tok_vsb1 = plan.step("vector",
                                     (lambda e, vtb=vtb:
                                      e.tensor_copy(out=vsb[0:32, 1024:2048],
                                                    in_=pbB[vtb][0:32, 0:1024])),
                                     waits=latest(tok_vt_f[1], prev_mm2), inc=sDV)
                vsb_war[vtb - 6] = tok_vsb1
                pchain[f"pb{vtb}"] = tok_vsb1
                tok_mx = plan.step("vector",
                                   (lambda e, scb=scb:
                                    e.reduce_max(out=mxs[0:32, :],
                                                 in_=pb[scb][0:32, :]
                                                 .rearrange("p (s t) -> p s t", t=32),
                                                 axis=AXX)),
                                   waits=(tok_mm1,), inc=sDV)
                tok_sub = plan.step("vector",
                                    (lambda e, scb=scb:
                                     e.tensor_tensor(out=sub_sb[0:32, :]
                                                     .rearrange("p (s t) -> p s t", t=32),
                                                     in0=pb[scb][0:32, :]
                                                     .rearrange("p (s t) -> p s t", t=32),
                                                     in1=mxs[0:32, :].unsqueeze(2)
                                                     .broadcast_to([32, 16, 32]),
                                                     op=ALU.subtract)),
                                    waits=latest(tok_mx, prev_exp), inc=sDV)
                sc_war[scb] = tok_sub
                pchain[f"pb{scb}"] = tok_sub
                tok_exp = plan.step("scalar",
                                    (lambda e: e.activation(out=prob[0:32, :],
                                                            in_=sub_sb[0:32, :], func=AF.Exp,
                                                            scale=float(SQD))),
                                    waits=latest(tok_sub, prev_pn), inc=sAC)
                prev_exp = tok_exp
                tok_sum = plan.step("vector",
                                    (lambda e: e.reduce_sum(out=dens[0:32, :],
                                                            in_=prob[0:32, :]
                                                            .rearrange("p (s t) -> p s t", t=32),
                                                            axis=AXX)),
                                    waits=(tok_exp,), inc=sDV)
                tok_rec = plan.step("vector",
                                    lambda e: e.reciprocal(out=recs[0:32, :], in_=dens[0:32, :]),
                                    waits=(tok_sum,), inc=sDV)
                tok_pn = plan.step("vector",
                                   (lambda e: e.tensor_tensor(out=pn[0:32, :]
                                                              .rearrange("p (s t) -> p s t", t=32),
                                                              in0=prob[0:32, :]
                                                              .rearrange("p (s t) -> p s t", t=32),
                                                              in1=recs[0:32, :].unsqueeze(2)
                                                              .broadcast_to([32, 16, 32]),
                                                              op=ALU.mult)),
                                   waits=(tok_rec,), inc=sDV)
                prev_pn = tok_pn
                tok_pt = plan.step("vector",
                                   lambda e: e.transpose(out=pt[0:32, :], in_=pn[0:32, :]),
                                   waits=latest(tok_pn, prev_mm2), inc=sDV)
                mm2_toks = []
                cse_now = []
                slotbase = (g % 2) * 16
                for p4 in range(4):
                    ctb = 2 + (p4 % 2)
                    tk = None
                    for i4 in range(4):
                        i = p4 * 4 + i4
                        wts = ()
                        if i4 == 0:
                            wts = latest(tok_pt, tok_vsb1, cse_chain[p4], cse_chain[p4 ^ 2])
                        tk = plan.step("tensor",
                                       (lambda e, i=i, i4=i4, ctb=ctb:
                                        e.matmul(pb[ctb][0:32, i4 * 128:(i4 + 1) * 128],
                                                 pt[0:32, i * 32:(i + 1) * 32],
                                                 vsb[0:32, i * 128:(i + 1) * 128],
                                                 start=(i4 == 0), stop=(i4 == 3))),
                                       waits=wts, inc=sPE if i4 == 3 else None)
                    mm2_toks.append(tk)
                    tok_cse = plan.step("scalar",
                                        (lambda e, p4=p4, slotbase=slotbase, ctb=ctb:
                                         e.copy(out=cs[0:32,
                                                       (slotbase + p4 * 4) * 128:
                                                       (slotbase + p4 * 4 + 4) * 128],
                                                in_=pb[ctb][0:32, :])),
                                        waits=latest(tk, *cs_guard), inc=sAC)
                    cse_chain[p4] = tok_cse
                    cse_now.append(tok_cse)
                prev_mm2 = latest(*mm2_toks)
                if g % 2 == 1:
                    rowbase = rt * RT + half * 256 + (g - 1) * 16
                    tok_cd = plan.dma("gpsimd",
                                      ctx_scr[rowbase:rowbase + 32, :]
                                      .rearrange("t (h d) -> h t d", d=128),
                                      cs[0:32, :].rearrange("p (t d) -> p t d", d=128),
                                      stream="cxw",
                                      waits=latest(*cse_now, cse_chain[0], cse_chain[1],
                                                   cse_chain[2], cse_chain[3]))
                    cs_guard = (tok_cd,)
                    ctx_toks.append(tok_cd)
            vth_guard = (tok_vt_f[1],)
        mix_end_guard = latest(sc_war[4], sc_war[5], cse_chain[0], cse_chain[1],
                               cse_chain[2], cse_chain[3], prev_mm2)
        last_vth["t"] = tok_vh
    pchain["pb2"] = latest(cse_chain[0], cse_chain[2])
    pchain["pb3"] = latest(cse_chain[1], cse_chain[3])
    tok_ctx_all = latest(*ctx_toks)

    out_toks = []
    tok_x2_all = None
    for rt in range(NRT):
        tok_T, tok_x2 = emit_ln_pass(rt, [("sync", x1_scr), ("scalar", ctx_scr)],
                                     ("gpsimd", x2_scr, "x2w"), lnT2, 0, RT,
                                     extra_w=(tok_x1_all, tok_ctx_all))
        tok_x2_all = tok_x2
        tok_n = plan.step("vector",
                          (lambda e: e.tensor_scalar(out=lnTn2[:, :], in0=lnT2[:, :],
                                                     scalar1=-1.0, scalar2=None, op0=ALU.mult)),
                          waits=(tok_T,), inc=sDV)
        tok_fT = emit_ffn(tok_T, tok_n, (), dreg_w=(tok_x2,),
                          lnT_t=lnT2, lnTn_t=lnTn2, dsln_t=dsln2)
        for sub in range(SUBRT):
            s = rt * SUBRT + sub
            sl = 0
            t1 = plan.dma("sync", xsub[sl][:, :], x2_scr[s * 128:(s + 1) * 128, :],
                          stream="xs", waits=tuple(xguard[sl]) + ((tok_x2, tok_fT) if sub == 0 else ()))
            xguard[sl] = ()
            t2 = plan.dma("scalar", ftsub[:, :], fT_scr[sub * 128:(sub + 1) * 128, :],
                          stream="fts", waits=tuple(ftguard["t"]) + ((tok_fT,) if sub == 0 else ()))
            t3 = plan.step("vector",
                           (lambda e, sl=sl: e.tensor_tensor(out=xsub[sl][:, :], in0=xsub[sl][:, :],
                                                             in1=ftsub[:, :], op=ALU.add)),
                           waits=(t1, t2), inc=sDV)
            ftguard["t"] = (t3,)
            t4 = plan.dma("gpsimd", out[s * 128:(s + 1) * 128, :], xsub[sl][:, :],
                          sem=sOUT, waits=(t3,))
            xguard[sl] = (t3, t4)
            out_toks.append(t4)

    if debug_taps:
        taps = [
            ("d_scr_g", scr_g), ("d_scr_u", scr_u), ("d_scr_d", scr_d),
            ("d_h", h_scr), ("d_fT", fT_scr),
            ("d_x1", x1_scr), ("d_x2", x2_scr), ("d_ctx", ctx_scr),
            ("d_vt", vt_scr),
        ]
        wall = (sOUT, sOUT.count)
        for nm, src in taps:
            dt = nc.dram_tensor(nm, list(src.shape), src.dtype, kind="ExternalOutput").ap()
            plan.dma("gpsimd", dt, src, sem=sOUT, waits=(wall,))
        for nm, t, w in (("d_scales", scales, 8), ("d_sval", sval, 16)):
            dt = nc.dram_tensor(nm, [t.shape[0], w], F32, kind="ExternalOutput").ap()
            plan.dma("gpsimd", dt, t[0:t.shape[0], 0:w], sem=sOUT, waits=(wall,))

    plan.step("gpsimd", None, waits=((sOUT, sOUT.count),))

    plan.realize()
    return nc


_cache = {}


def kernel(L0_attn_hidden_states, L0_attn_wq, L0_attn_bq, L0_attn_wk, L0_attn_bk,
           L0_attn_wv, L0_attn_bv, L0_attn_wl, L0_attn_bl,
           L0_ffn_w_gate, L0_ffn_w_up, L0_ffn_w_down):
    x = np.ascontiguousarray(np.asarray(L0_attn_hidden_states).reshape(8192, 4096))
    args = {
        "wq": np.ascontiguousarray(np.asarray(L0_attn_wq)),
        "wk": np.ascontiguousarray(np.asarray(L0_attn_wk)),
        "wv": np.ascontiguousarray(np.asarray(L0_attn_wv)),
        "bq": np.ascontiguousarray(np.asarray(L0_attn_bq)),
        "bk": np.ascontiguousarray(np.asarray(L0_attn_bk)),
        "bv": np.ascontiguousarray(np.asarray(L0_attn_bv)),
        "wg": np.ascontiguousarray(np.asarray(L0_ffn_w_gate)),
        "wu": np.ascontiguousarray(np.asarray(L0_ffn_w_up)),
        "wd": np.ascontiguousarray(np.asarray(L0_ffn_w_down)),
    }
    if "nc" not in _cache:
        _cache["nc"] = build_program()
    nc = _cache["nc"]
    in_maps = [dict(args, x0=x[c * RPC:(c + 1) * RPC]) for c in range(NCORES)]
    res = run_bass_kernel_spmd(nc, in_maps, core_ids=list(range(NCORES)))
    outs = [res.results[c]["out"] for c in range(NCORES)]
    return np.concatenate(outs, axis=0).reshape(4, 2048, 4096)


if __name__ == "__main__":
    import time
    t0 = time.time()
    nc = build_program()
    print("built ok in %.1fs; instructions: %d" % (time.time() - t0, nc.n_instructions()))


# version 07
# speedup vs baseline: 1.0272x; 1.0272x over previous
import math
import numpy as np
import ml_dtypes
import concourse.bass as bass
import concourse.mybir as mybir
from concourse.bass_utils import run_bass_kernel_spmd

F32 = mybir.dt.float32
BF16 = mybir.dt.bfloat16
AF = mybir.ActivationFunctionType
ALU = mybir.AluOpType
AXX = mybir.AxisListType.X

NCORES = 8
D = 4096
RPC = 1024
NSUB = RPC // 128
RT = 512
NRT = RPC // RT
SUBRT = RT // 128
ICC = D // 128
H8 = 1366
D8 = 512
KB = 8
CB = 11
HC = KB * CB
NHEAD = 32
HD = 128
SQD = 1.0 / math.sqrt(HD)
B0 = 20.0
EPS = 1e-5
NW_GU = 8 * H8 * D8


def _oct_tables():
    TRIPLES = [(1, 2, 3), (1, 4, 5), (1, 7, 6), (2, 4, 6), (2, 5, 7), (3, 4, 7), (3, 6, 5)]
    S = np.zeros((8, 8, 8), np.float32)
    S[0, 0, 0] = 1.0
    for i in range(1, 8):
        S[0, i, i] = 1.0
        S[i, 0, i] = 1.0
        S[i, i, 0] = -1.0
    for a, b, c in TRIPLES:
        for x, y, z in ((a, b, c), (b, c, a), (c, a, b)):
            S[x, y, z] = 1.0
            S[y, x, z] = -1.0
    J = [[None] * 8 for _ in range(8)]
    G = [[0.0] * 8 for _ in range(8)]
    for i in range(8):
        for k in range(8):
            js = [j for j in range(8) if S[j, i, k] != 0]
            assert len(js) == 1
            J[i][k] = js[0]
            G[i][k] = float(S[js[0], i, k])
    return J, G


OCT_J, OCT_G = _oct_tables()


class Sem:
    __slots__ = ("h", "count", "name")

    def __init__(self, nc, name):
        self.h = nc.alloc_semaphore(name)
        self.count = 0
        self.name = name


class Plan:
    ENGINES = ("sync", "scalar", "tensor", "vector", "gpsimd")

    def __init__(self, nc):
        self.nc = nc
        self.q = {e: [] for e in self.ENGINES}
        self._sems = {}

    def sem(self, name):
        if name not in self._sems:
            self._sems[name] = Sem(self.nc, name)
        return self._sems[name]

    def step(self, eng, fn, waits=(), inc=None, inc_amt=1):
        tok = None
        if inc is not None:
            inc.count += inc_amt
            tok = (inc, inc.count)
        ws = latest(*waits)
        self.q[eng].append((ws, fn, tok, inc_amt))
        return tok

    def dma(self, eng, out, in_, waits=(), sem=None, stream=None, **kw):
        if sem is None:
            assert stream is not None
            sem = self.sem("D_" + stream)
        return self.step(eng, lambda e: e.dma_start(out=out, in_=in_, **kw),
                         waits=waits, inc=sem, inc_amt=16)

    def realize(self):
        streams = self.q
        with self.nc.Block() as block:
            def mk(ename):
                def body(eng):
                    waited = {}
                    for ws, fn, tok, amt in streams[ename]:
                        for (s, v) in ws:
                            if waited.get(s.name, 0) >= v:
                                continue
                            eng.wait_ge(s.h, v)
                            waited[s.name] = v
                        ins = fn(eng) if fn is not None else None
                        if tok is not None:
                            (ins if ins is not None else eng.nop()).then_inc(tok[0].h, amt)
                return body
            block.sync(mk("sync"))
            block.scalar(mk("scalar"))
            block.tensor(mk("tensor"))
            block.vector(mk("vector"))
            block.gpsimd(mk("gpsimd"))


def latest(*toks):
    best = {}

    def add(t):
        if t is None:
            return
        if isinstance(t, tuple) and len(t) == 2 and isinstance(t[0], Sem):
            s, v = t
            if s.name not in best or best[s.name][1] < v:
                best[s.name] = (s, v)
        else:
            for x in t:
                add(x)

    for t in toks:
        add(t)
    return tuple(best.values())


def build_program(debug_taps=False):
    nc = bass.Bass("TRN2", target_bir_lowering=False, debug=False)

    x0 = nc.dram_tensor("x0", [RPC, D], F32, kind="ExternalInput").ap()
    wq = nc.dram_tensor("wq", [D, D], F32, kind="ExternalInput").ap()
    wk = nc.dram_tensor("wk", [D, D], F32, kind="ExternalInput").ap()
    wv = nc.dram_tensor("wv", [D, D], F32, kind="ExternalInput").ap()
    bq = nc.dram_tensor("bq", [D], F32, kind="ExternalInput").ap()
    bk = nc.dram_tensor("bk", [D], F32, kind="ExternalInput").ap()
    bv = nc.dram_tensor("bv", [D], F32, kind="ExternalInput").ap()
    wg = nc.dram_tensor("wg", [8, H8, D8], F32, kind="ExternalInput").ap()
    wu = nc.dram_tensor("wu", [8, H8, D8], F32, kind="ExternalInput").ap()
    wd = nc.dram_tensor("wd", [8, D8, H8], F32, kind="ExternalInput").ap()
    out = nc.dram_tensor("out", [RPC, D], F32, kind="ExternalOutput").ap()

    wqB = nc.dram_tensor("wqB", [D, D], BF16).ap()
    wkB = nc.dram_tensor("wkB", [D, D], BF16).ap()
    wvB = nc.dram_tensor("wvB", [D, D], BF16).ap()
    wqT = nc.dram_tensor("wqT", [D, D], BF16).ap()
    wkT = nc.dram_tensor("wkT", [D, D], BF16).ap()
    wvT = nc.dram_tensor("wvT", [D, D], BF16).ap()
    scr_g = nc.dram_tensor("scr_g", [CB, 128, KB, 4, 128], BF16).ap()
    scr_u = nc.dram_tensor("scr_u", [CB, 128, KB, 4, 128], BF16).ap()
    scr_d = nc.dram_tensor("scr_d", [4, 128, KB, CB, 128], BF16).ap()
    h_scr = nc.dram_tensor("h_scr", [HC, 128, RT], BF16).ap()
    fT_scr = nc.dram_tensor("fT_scr", [RT, D], BF16).ap()
    x1_scr = nc.dram_tensor("x1_scr", [RPC, D], F32).ap()
    x2_scr = nc.dram_tensor("x2_scr", [RPC, D], F32).ap()
    ctx_scr = nc.dram_tensor("ctx_scr", [RPC, D], BF16).ap()
    vt_scr = nc.dram_tensor("vt_scr", [D, RT], BF16).ap()

    identD = nc.inline_tensor(np.eye(128, dtype=ml_dtypes.bfloat16), name="identc").ap()
    ones1fD = nc.inline_tensor(np.ones((1, 128), np.float32), name="ones1fc").ap()
    ones1bD = nc.inline_tensor(np.ones((1, 128), ml_dtypes.bfloat16), name="ones1bc").ap()
    onescbD = nc.inline_tensor(np.ones((128, 1), ml_dtypes.bfloat16), name="onescbc").ap()
    onescfD = nc.inline_tensor(np.ones((128, 1), np.float32), name="onescfc").ap()
    constsD = nc.inline_tensor(
        np.tile(np.array([[0.5, -0.5, EPS, 0.0]], np.float32), (128, 1)), name="constsc").ap()

    ident = nc.alloc_sbuf_tensor("ident", [128, 128], BF16)
    ones1f = nc.alloc_sbuf_tensor("ones1f_s", [1, 128], F32)
    ones1b = nc.alloc_sbuf_tensor("ones1b_s", [1, 128], BF16)
    onescb = nc.alloc_sbuf_tensor("onescb_s", [128, 1], BF16)
    onescf = nc.alloc_sbuf_tensor("onescf_s", [128, 1], F32)
    consts = nc.alloc_sbuf_tensor("consts", [128, 4], F32)
    scales = nc.alloc_sbuf_tensor("scales", [128, 8], F32)
    C_RG, C_RU, C_RD, C_SG2, C_SU2, C_SD4, C_EB = range(7)
    sstat = nc.alloc_sbuf_tensor("sstat", [128, 8], F32)
    redq = nc.alloc_sbuf_tensor("redq", [128, 112], F32)
    sval = nc.alloc_sbuf_tensor("sval", [1, 16], F32)
    bqT = nc.alloc_sbuf_tensor("bqT", [128, ICC], F32)
    bkT = nc.alloc_sbuf_tensor("bkT", [128, ICC], F32)
    bvT = nc.alloc_sbuf_tensor("bvT", [128, ICC], F32)
    nrmio = nc.alloc_sbuf_tensor("nrmio", [1, 1], F32)

    offA, _ = nc.bump_sbuf(65536)
    offB, _ = nc.bump_sbuf(65536)
    offC, _ = nc.bump_sbuf(32768)
    offD, _ = nc.bump_sbuf(24576)
    offE, _ = nc.bump_sbuf(16384)

    def at(name, shape, dtype, base, off):
        return nc.alloc_sbuf_tensor_at(name, shape, dtype, offset=base + off)

    lnT = at("lnT", [128, ICC * RT], BF16, offA, 0)
    lnTn = at("lnTn", [128, ICC * RT], BF16, offA, 32768)
    qTr = at("qTr", [128, ICC * RT], BF16, offA, 0)
    kTr = at("kTr", [128, ICC * RT], BF16, offA, 32768)
    qsrc = [at(f"qsrc{i}", [128, 1408], F32, offB, i * 5632) for i in range(2)]
    qtq = at("qtq", [128, 1408], BF16, offB, 11264)
    qtq2 = at("qtq2", [128, 1408], BF16, offB, 14080)
    qstag = at("qstag", [128, 1408], BF16, offB, 16896)
    ln2T = at("ln2T", [128, ICC * RPC], BF16, offB, 0)
    lnT2 = at("lnT2", [128, ICC * RT], BF16, offB, 0)
    lnTn2 = at("lnTn2", [128, ICC * RT], BF16, offB, 32768)
    dsln2 = at("dsln2", [128, HC * 128], BF16, offB, 0)

    gsl = [at(f"gsl{i}", [128, 4096], BF16, offC, i * 8192) for i in range(2)]
    usl = [at(f"usl{i}", [128, 4096], BF16, offC, 16384 + i * 8192) for i in range(2)]
    dsl = at("dsl", [128, HC * 128], BF16, offC, 0)
    dsln = at("dsln", [128, HC * 128], BF16, offA, 0)
    sqs = at("sqs", [128, D], BF16, offC, 0)
    ftsub = at("ftsub", [128, D], BF16, offC, 8192)
    wst = [at(f"wst{i}", [128, ICC * 128], BF16, offC, i * 8192) for i in range(2)]
    wvcol = at("wvcol", [128, ICC * 256], BF16, offC, 16384)
    xsub = [at("xsub0", [128, D], F32, offD, 0)]
    hst = at("hst", [128, 4 * RT], BF16, offD, 0)
    silu_t = at("silu_t", [128, RT], F32, offD, 4096)
    dstag = at("dstag", [128, RT], BF16, offD, 6144)
    fstag = at("fstag", [128, 512], BF16, offD, 7168)
    hrd = [at(f"hrd{i}", [128, 8 * RT], BF16, offD, 8192 + i * 8192) for i in range(2)]

    vthalf = at("vthalf", [128, ICC * 256], BF16, offD, 0)
    cs = at("cs", [32, 32 * 128], BF16, offD, 16384)
    brow = at("brow", [1, D], F32, offE, 0)
    tstag = at("tstag", [128, D], BF16, offE, 0)
    sub_sb = at("sub_sb", [32, 512], F32, offE, 0)
    prob = at("prob", [32, 512], F32, offE, 2048)
    pn = at("pn", [32, 512], BF16, offE, 4096)
    pt = at("pt", [32, 512], BF16, offE, 5120)
    vsb = at("vsb", [32, 16 * 128], BF16, offE, 6144)
    mxs = at("mxs", [32, 16], F32, offE, 10240)
    vstg = at("vstg", [128, RT], BF16, offE, 11264)
    dens = at("dens", [32, 16], F32, offE, 10496)
    recs = at("recs", [32, 16], F32, offE, 10752)
    qscu = at("qscu", [128, 2048], F32, offE, 8192)

    pb = [nc.alloc_psum_tensor(f"pb{i}", [128, 512], F32) for i in range(8)]
    pbB = [p.ap().bitcast(BF16) for p in pb]

    plan = Plan(nc)
    sIN = plan.sem("sIN")
    sSC = plan.sem("sSC")
    sGP = plan.sem("sGP")
    sPE = plan.sem("sPE")
    sDV = plan.sem("sDV")
    sAC = plan.sem("sAC")
    sCC = plan.sem("sCC")
    sOUT = plan.sem("sOUT")

    xguard = {0: (), 1: ()}
    ftguard = {"t": ()}

    cast_toks = {}
    cast_steps = []
    for name, (srcm, dstB) in {"v": (wv, wvB), "q": (wq, wqB), "k": (wk, wkB)}.items():
        for ch in range(16):
            def go_cast(name=name, srcm=srcm, dstB=dstB, ch=ch):
                cast_toks[name] = plan.dma(
                    "gpsimd", dstB[ch * 256:(ch + 1) * 256, :],
                    srcm[ch * 256:(ch + 1) * 256, :], stream="cast_" + name)
            cast_steps.append(go_cast)

    tcl = []
    for dst, src in ((ident, identD), (ones1f, ones1fD), (ones1b, ones1bD),
                     (onescb, onescbD), (onescf, onescfD), (consts, constsD)):
        sh = dst.shape
        tcl.append(plan.dma("sync", dst[0:sh[0], 0:sh[1]], src, stream="const"))
    tok_consts = latest(*tcl)

    def bias_transpose(bsrc, bdst, extra):
        tok_l = plan.dma("sync", brow[0:1, :], bsrc[None, :], stream="brow", waits=extra)
        tok_mm = None
        for ch in range(ICC):
            tok_mm = plan.step("tensor",
                               (lambda e, ch=ch: e.matmul(pb[6][:, ch:ch + 1],
                                                          brow[0:1, ch * 128:(ch + 1) * 128],
                                                          ones1f[0:1, 0:1],
                                                          start=(ch == 0), stop=(ch == ICC - 1))),
                               waits=(tok_l, tok_consts) if ch == 0 else (),
                               inc=sPE if ch == ICC - 1 else None)
        tok_ev = plan.step("vector",
                           (lambda e: e.tensor_copy(out=bdst[0:128, 0:ICC], in_=pb[6][:, 0:ICC])),
                           waits=(tok_mm,), inc=sDV)
        return tok_ev

    tok_bqT = bias_transpose(bq, bqT, ())
    tok_bkT = bias_transpose(bk, bkT, (tok_bqT,))
    tok_bvT = bias_transpose(bv, bvT, (tok_bkT,))

    tok_ms_qstag = plan.step("vector", lambda e: e.memset(qstag[:, :], 0.0), inc=sDV)

    qsc_g = [usl[0].ap().bitcast(F32), usl[1].ap().bitcast(F32)]
    qsc_u = [qscu[:, 0:1024], qscu[:, 1024:2048]]

    def quant_scale(wsrc, nelem, col_recip, col_out, out_div, queue, bufs, bufw,
                    guards, rqbase, use_act):
        flat = wsrc.rearrange("a b c -> (a b c)").rearrange("(p n) -> p n", p=128)
        percol = nelem // 128
        tok_r = None
        ci = 0
        pos = 0
        while pos < percol:
            cw = min(bufw, percol - pos)
            sl = ci % 2
            tok_l = plan.dma(queue, bufs[sl][:, 0:cw], flat[:, pos:pos + cw],
                             stream=f"qsc{queue}{sl}", waits=tuple(guards[sl]))
            if use_act:
                tok_r = plan.step("scalar",
                                  (lambda e, sl=sl, cw=cw, ci=ci:
                                   e.activation(out=sqs[:, 0:cw].bitcast(F32), in_=bufs[sl][:, 0:cw],
                                                func=AF.Abs,
                                                accum_out=redq[:, rqbase + ci:rqbase + ci + 1])),
                                  waits=(tok_l,), inc=sAC)
            else:
                tok_r = plan.step("vector",
                                  (lambda e, sl=sl, cw=cw, ci=ci:
                                   e.tensor_reduce(out=redq[:, rqbase + ci:rqbase + ci + 1],
                                                   in_=bufs[sl][:, 0:cw],
                                                   axis=AXX, op=ALU.add, apply_absolute_value=True)),
                                  waits=(tok_l,), inc=sDV)
            guards[sl] = (tok_r,)
            pos += cw
            ci += 1
        tok_r2 = plan.step("vector",
                           (lambda e, ci=ci:
                            e.tensor_reduce(out=redq[:, rqbase + ci:rqbase + ci + 1],
                                            in_=redq[:, rqbase:rqbase + ci],
                                            axis=AXX, op=ALU.add)),
                           waits=tuple(guards[0]) + tuple(guards[1]), inc=sDV)
        tok_mm = plan.step("tensor",
                           (lambda e, ci=ci: e.matmul(pb[6][0:1, 256:257],
                                                      redq[:, rqbase + ci:rqbase + ci + 1],
                                                      onescf[:, 0:1],
                                               start=True, stop=True)),
                           waits=(tok_r2,), inc=sPE)
        tok_s = plan.step("vector",
                          lambda e: e.tensor_copy(out=sval[0:1, 0:1], in_=pb[6][0:1, 256:257]),
                          waits=(tok_mm,), inc=sDV)
        tok_rc = plan.step("vector", lambda e: e.reciprocal(out=sval[0:1, 1:2], in_=sval[0:1, 0:1]),
                           waits=(tok_s,), inc=sDV)
        tok_rs = plan.step("vector",
                           (lambda e: e.tensor_scalar(out=sval[0:1, 2:3], in0=sval[0:1, 1:2],
                                                      scalar1=float(nelem), scalar2=None, op0=ALU.mult)),
                           waits=(tok_rc,), inc=sDV)
        tok_os = plan.step("vector",
                           (lambda e: e.tensor_scalar(out=sval[0:1, 3:4], in0=sval[0:1, 0:1],
                                                      scalar1=float(1.0 / (nelem * out_div)),
                                                      scalar2=None, op0=ALU.mult)),
                           waits=(tok_s,), inc=sDV)
        tok_b1 = plan.step("tensor",
                           (lambda e: e.matmul(pb[6][:, 257:258], ones1f[0:1, :], sval[0:1, 2:3],
                                               start=True, stop=False)),
                           waits=(tok_rs, tok_os), inc=None)
        tok_b2 = plan.step("tensor",
                           (lambda e: e.matmul(pb[6][:, 258:259], ones1f[0:1, :], sval[0:1, 3:4],
                                               start=False, stop=True)),
                           waits=(), inc=sPE)
        tok_e1 = plan.step("vector",
                           (lambda e, col_recip=col_recip:
                            e.tensor_copy(out=scales[:, col_recip:col_recip + 1],
                                          in_=pb[6][:, 257:258])),
                           waits=(tok_b2,), inc=sDV)
        tok_e2 = plan.step("vector",
                           (lambda e, col_out=col_out:
                            e.tensor_copy(out=scales[:, col_out:col_out + 1],
                                          in_=pb[6][:, 258:259])),
                           waits=(tok_b2,), inc=sDV)
        return latest(tok_e1, tok_e2)

    pchain = {"pb6": None, "pb7": None, "pb4": None, "pb5": None,
              "pb2": None, "pb3": None}
    qchain = {"mm": None, "w": None}
    qslot = [(), ()]

    def quant_ternary(src_ap, rows, cols, col_recip, wts):
        tok_s1 = plan.step("scalar",
                           (lambda e: e.activation(out=qtq[0:rows, 0:cols], in_=src_ap,
                                                   func=AF.Sign, bias=consts[0:rows, 0:1],
                                                   scale=scales[0:rows, col_recip:col_recip + 1])),
                           waits=wts, inc=sAC)
        tok_s2 = plan.step("scalar",
                           (lambda e: e.activation(out=qtq2[0:rows, 0:cols], in_=src_ap,
                                                   func=AF.Sign, bias=consts[0:rows, 1:2],
                                                   scale=scales[0:rows, col_recip:col_recip + 1])),
                           waits=(tok_s1,), inc=sAC)
        tok_t = plan.step("vector",
                          (lambda e: e.tensor_tensor(out=qtq[0:rows, 0:cols],
                                                     in0=qtq[0:rows, 0:cols],
                                                     in1=qtq2[0:rows, 0:cols], op=ALU.add)),
                          waits=(tok_s2,), inc=sDV)
        return tok_t

    qctr = {"n": 0}
    scr_g_done = {}
    scr_u_done = {}
    scr_d_done = {}

    def quant_gu_step(wsrc, scr, col_recip, tok_scale, oct_, j, done):
        rows = 128 if oct_ < 10 else H8 - 1280
        sl = qctr["n"] % 2
        qctr["n"] += 1
        tok_l = plan.dma("sync", qsrc[sl][0:rows, 0:D8],
                         wsrc[j, oct_ * 128:oct_ * 128 + rows, :],
                         stream=f"qs{sl}", waits=qslot[sl])
        tok_t = quant_ternary(qsrc[sl][0:rows, 0:D8], rows, D8, col_recip,
                              (tok_l, tok_scale, tok_ms_qstag, qchain["mm"]))
        qslot[sl] = (tok_t,)
        tok_mm = None
        for icc4 in range(4):
            tok_mm = plan.step("tensor",
                               (lambda e, icc4=icc4, rows=rows:
                                e.matmul(pbB[7][:, icc4 * 128:icc4 * 128 + rows],
                                         qtq[0:rows, icc4 * 128:(icc4 + 1) * 128],
                                         ident[0:rows, 0:rows],
                                         start=(icc4 == 0), stop=(icc4 == 3),
                                         is_transpose=True)),
                               waits=latest(tok_t, pchain["pb7"]) if icc4 == 0 else (),
                               inc=sPE if icc4 == 3 else None)
        qchain["mm"] = tok_mm
        tok_pad = None
        if rows < 128:
            tok_pad = plan.step("vector",
                                (lambda e, rows=rows:
                                 e.memset(qstag[:, 0:512]
                                          .rearrange("p (i c) -> p i c", i=4)[:, :, rows:128],
                                          0.0)),
                                waits=latest(qchain["w"],), inc=sDV)
        tok_ev = plan.step("vector",
                           (lambda e, rows=rows:
                            e.tensor_copy(
                                out=qstag[:, 0:512].rearrange("p (i c) -> p i c", i=4)[:, :, 0:rows],
                                in_=pbB[7][:, 0:512].rearrange("p (i c) -> p i c", i=4)[:, :, 0:rows])),
                           waits=latest(tok_mm, qchain["w"], tok_pad), inc=sDV)
        pchain["pb7"] = tok_ev
        tok_w = plan.dma("scalar", scr[oct_, :, j, :, :],
                         qstag[:, 0:512].rearrange("p (i c) -> p i c", i=4),
                         stream="wscr", waits=latest(tok_ev, tok_pad))
        qchain["w"] = tok_w
        if j == 7:
            done[oct_] = tok_w

    def quant_d_step(tok_scale, w, j):
        sl = qctr["n"] % 2
        qctr["n"] += 1
        tok_l = plan.dma("sync", qsrc[sl][:, 0:H8], wd[j, w * 128:(w + 1) * 128, :],
                         stream=f"qs{sl}", waits=qslot[sl])
        if True:
            if True:
                tok_t = quant_ternary(qsrc[sl][:, 0:H8], 128, H8, C_RD,
                                      (tok_l, tok_scale, tok_ms_qstag, qchain["mm"]))
                qslot[sl] = (tok_t,)
                tok7 = tok6 = None
                for dc in list(range(8)) + [8, 10, 9]:
                    dw = 128 if dc < 10 else H8 - 1280
                    if dc < 8:
                        bank, pos = pbB[7], dc
                        first, lastc = dc == 0, dc == 7
                        chain = pchain["pb7"]
                    else:
                        bank, pos = pbB[6], dc - 8
                        first, lastc = dc == 8, dc == 9
                        chain = pchain["pb6"]
                    wts = latest(tok_t, chain) if first else ()
                    tk = plan.step("tensor",
                                   (lambda e, dc=dc, dw=dw, bank=bank, pos=pos,
                                           first=first, lastc=lastc:
                                    e.matmul(bank[0:dw, pos * 128:(pos + 1) * 128],
                                             qtq[:, dc * 128:dc * 128 + dw],
                                             ident[:, :],
                                             start=first, stop=lastc, is_transpose=True)),
                                   waits=wts, inc=sPE if lastc else None)
                    if dc == 7:
                        tok7 = tk
                    if dc == 9:
                        tok6 = tk
                qchain["mm"] = latest(tok7, tok6)
                tok_e1 = plan.step("vector",
                                   lambda e: e.tensor_copy(out=qstag[:, 0:1024], in_=pbB[7][:, 0:1024]),
                                   waits=latest(tok7, qchain["w"]), inc=sDV)
                tok_e2 = plan.step("vector",
                                   lambda e: e.tensor_copy(out=qstag[:, 1024:1280], in_=pbB[6][:, 0:256]),
                                   waits=(tok6,), inc=sDV)
                tok_e3 = plan.step("vector",
                                   (lambda e: e.tensor_copy(out=qstag[0:H8 - 1280, 1280:1408],
                                                            in_=pbB[6][0:H8 - 1280, 256:384])),
                                   waits=(tok6,), inc=sDV)
                pchain["pb7"] = tok_e1
                pchain["pb6"] = tok_e3
                tok_w = plan.dma("scalar", scr_d[w, :, j, :, :],
                                 qstag[:, 0:CB * 128].rearrange("p (i c) -> p i c", i=CB),
                                 stream="wscr", waits=(tok_e1, tok_e2, tok_e3))
                qchain["w"] = tok_w
        if j == 7:
            scr_d_done[w] = tok_w

    quant_steps = []
    Q_TOTAL = 16 * CB + 32

    def emit_scales_and_steps():
        tok_sg = quant_scale(wg, NW_GU, C_RG, C_SG2, 2.0, "sync", qsc_g, 2048,
                             [(), ()], 0, False)
        tok_su = quant_scale(wu, NW_GU, C_RU, C_SU2, 2.0, "scalar", qsc_u, 1024,
                             [(), ()], 24, False)
        tok_sd = quant_scale(wd, NW_GU, C_RD, C_SD4, 4.0, "gpsimd", qsrc, 1408,
                             qslot, 70, False)
        for c_ in range(CB):
            for j_ in range(8):
                quant_steps.append(lambda c_=c_, j_=j_, ts=tok_sg:
                                   quant_gu_step(wg, scr_g, C_RG, ts, c_, j_, scr_g_done))
            for j_ in range(8):
                quant_steps.append(lambda c_=c_, j_=j_, ts=tok_su:
                                   quant_gu_step(wu, scr_u, C_RU, ts, c_, j_, scr_u_done))
        for w_ in range(4):
            for j_ in range(8):
                quant_steps.append(lambda w_=w_, j_=j_, ts=tok_sd: quant_d_step(ts, w_, j_))
        assert len(quant_steps) == Q_TOTAL

    def pump_quant(n):
        for _ in range(n):
            if quant_steps:
                quant_steps.pop(0)()

    def pump_quant_to(n):
        while quant_steps and (Q_TOTAL - len(quant_steps)) < n:
            quant_steps.pop(0)()

    prep_steps = list(cast_steps)
    prep_state = {"tstag_free": (tok_bvT,), "last": None}

    for (mB, mT, ckey) in ((wvB, wvT, "v"), (wqB, wqT, "q"), (wkB, wkT, "k")):
        for icc in range(ICC):
            def go(mB=mB, mT=mT, ckey=ckey, icc=icc):
                t1 = plan.dma("scalar", tstag[:, :], mB[:, icc * 128:(icc + 1) * 128],
                              stream="prin", waits=(cast_toks[ckey],) + tuple(prep_state["tstag_free"]),
                              transpose=True)
                t2 = plan.dma("scalar", mT[icc * 128:(icc + 1) * 128, :], tstag[:, :],
                              stream="prout", waits=(t1,))
                prep_state["tstag_free"] = (t2,)
                prep_state["last"] = t2
                prep_state["done_" + ckey] = t2
            prep_steps.append(go)

    def pump_prep(n):
        for _ in range(n):
            if prep_steps:
                prep_steps.pop(0)()

    lnpass = {"evac4": None, "evac5": None, "sqs_guard": None}

    def emit_ln_pass(rt, srcs, spill_to, T_dst, T_rowbase, T_rowspan, extra_w=()):
        tok_T_all = []
        tok_spill = []
        for sub in range(SUBRT):
            s = rt * SUBRT + sub
            sl = 0
            q0, ap0 = srcs[0]
            tok_l0 = plan.dma(q0, xsub[sl][:, :], ap0[s * 128:(s + 1) * 128, :],
                              stream="xs", waits=tuple(xguard[sl]) + tuple(extra_w))
            xguard[sl] = ()
            tok_add = tok_l0
            if len(srcs) > 1:
                q1, ap1 = srcs[1]
                assert ap1.dtype == BF16
                dst1, g1 = ftsub, None
                gw = tuple(ftguard["t"])
                r1 = sub if ap1.shape[0] == RT else s
                tok_l1 = plan.dma(q1, dst1[:, :], ap1[r1 * 128:(r1 + 1) * 128, :],
                                  stream="fts", waits=gw + tuple(extra_w))
                tok_add = plan.step("vector",
                                    (lambda e, sl=sl, dst1=dst1:
                                     e.tensor_tensor(out=xsub[sl][:, :], in0=xsub[sl][:, :],
                                                     in1=dst1[:, :], op=ALU.add)),
                                    waits=(tok_l0, tok_l1), inc=sDV)
                if g1 is not None:
                    xguard[g1] = (tok_add,)
                else:
                    ftguard["t"] = (tok_add,)
            if spill_to is not None:
                qs, aps, spill_stream = spill_to
                tok_sp = plan.dma(qs, aps[s * 128:(s + 1) * 128, :], xsub[sl][:, :],
                                  stream=spill_stream, waits=(tok_add,))
                tok_spill.append(tok_sp)
            tok_sum = plan.step("vector",
                                (lambda e, sl=sl: e.reduce_sum(out=sstat[:, 0:1],
                                                               in_=xsub[sl][:, :], axis=AXX)),
                                waits=(tok_add,), inc=sDV)
            tok_sq = plan.step("scalar",
                               (lambda e, sl=sl: e.activation(out=sqs[:, :], in_=xsub[sl][:, :],
                                                              func=AF.Square,
                                                              accum_out=sstat[:, 1:2])),
                               waits=latest(tok_add, lnpass["sqs_guard"]), inc=sAC)
            tok_m = plan.step("vector",
                              (lambda e: e.tensor_scalar(out=sstat[:, 2:3], in0=sstat[:, 0:1],
                                                         scalar1=float(-1.0 / D), scalar2=None,
                                                         op0=ALU.mult)),
                              waits=(tok_sum,), inc=sDV)
            tok_m2 = plan.step("scalar",
                               (lambda e: e.activation(out=sstat[:, 3:4], in_=sstat[:, 2:3],
                                                       func=AF.Square)),
                               waits=(tok_m,), inc=sAC)
            tok_v = plan.step("vector",
                              (lambda e: e.scalar_tensor_tensor(out=sstat[:, 4:5], in0=sstat[:, 1:2],
                                                                scalar=float(1.0 / D),
                                                                in1=sstat[:, 3:4],
                                                                op0=ALU.mult, op1=ALU.subtract)),
                              waits=(tok_sq, tok_m2), inc=sDV)
            tok_lnv = plan.step("scalar",
                                (lambda e: e.activation(out=sstat[:, 5:6], in_=sstat[:, 4:5],
                                                        func=AF.Ln, bias=consts[:, 2:3])),
                                waits=(tok_v,), inc=sAC)
            tok_rstd = plan.step("scalar",
                                 (lambda e: e.activation(out=sstat[:, 6:7], in_=sstat[:, 5:6],
                                                         func=AF.Exp, scale=-0.5)),
                                 waits=(tok_lnv,), inc=sAC)
            tok_ap = plan.step("vector",
                               (lambda e, sl=sl: e.tensor_scalar(out=sqs[:, :], in0=xsub[sl][:, :],
                                                                 scalar1=sstat[:, 2:3],
                                                                 scalar2=sstat[:, 6:7],
                                                                 op0=ALU.add, op1=ALU.mult)),
                               waits=(tok_rstd, tok_sq), inc=sDV)
            xguard[sl] = xguard[sl] + (tok_ap,) + tuple(tok_spill[-1:] if spill_to else ())
            for pk in range(4):
                bank = 4 + (pk % 2)
                tok_mm = None
                for i8 in range(8):
                    icc = pk * 8 + i8
                    wts = latest(tok_ap, lnpass[f"evac{bank}"], pchain[f"pb{bank}"]) if i8 == 0 else ()
                    tok_mm = plan.step("tensor",
                                       (lambda e, bank=bank, i8=i8, icc=icc:
                                        e.matmul(pbB[bank][:, i8 * 128:(i8 + 1) * 128],
                                                 sqs[:, icc * 128:(icc + 1) * 128],
                                                 ident[:, :], start=(i8 == 0), stop=(i8 == 7),
                                                 is_transpose=True)),
                                       waits=wts, inc=sPE if i8 == 7 else None)
                tok_ev = plan.step("vector",
                                   (lambda e, bank=bank, pk=pk, sub=sub:
                                    e.tensor_copy(
                                        out=T_dst[:, :].rearrange("p (i r) -> p i r", r=T_rowspan)
                                        [:, pk * 8:(pk + 1) * 8,
                                         T_rowbase + sub * 128:T_rowbase + (sub + 1) * 128],
                                        in_=pbB[bank][:, 0:1024].rearrange("p (i r) -> p i r", r=128))),
                                   waits=(tok_mm,), inc=sDV)
                lnpass[f"evac{bank}"] = tok_ev
                pchain[f"pb{bank}"] = tok_ev
                tok_T_all.append(tok_ev)
            lnpass["sqs_guard"] = latest(*tok_T_all[-4:])
        return latest(*tok_T_all), (latest(*tok_spill) if tok_spill else None)

    def emit_ffn(tok_lnT, tok_lnTn, first_w, first_wu=(), first_wd=(), dreg_w=(),
                 lnT_t=None, lnTn_t=None, dsln_t=None, qp=False):
        lnT_t = lnT_t if lnT_t is not None else lnT
        lnTn_t = lnTn_t if lnTn_t is not None else lnTn
        dsln_t = dsln_t if dsln_t is not None else dsln
        """first_w: extra waits for the first weight-slice loads of this call.
        qp: pump the quant queue alongside this call (FFN1 rowtile 0)."""
        evac_toks = {0: None, 1: None, 2: None, 3: None}
        gu_slot_free = [(), ()]
        hfree2 = {0: (), 1: ()}
        hdma_toks = []
        half_toks = []
        prev_tok_h = None
        for c in range(CB):
            if qp:
                pump_quant_to(16 * (c + 1))
            sl = c % 2
            lw = tuple(gu_slot_free[sl])
            if c < 2:
                lw = lw + tuple(first_w) + (tok_lnT,)
            if qp:
                lw = lw + (scr_g_done[c],)
            tg = plan.dma("sync", gsl[sl][:, :],
                          scr_g[c].rearrange("p a b c -> p (a b c)"), stream=f"gsl{sl}", waits=lw)
            tu = plan.dma("sync", usl[sl][:, :],
                          scr_u[c].rearrange("p a b c -> p (a b c)"), stream=f"usl{sl}",
                          waits=tuple(gu_slot_free[sl]) + (tuple(first_wu) if c < 2 else ())
                          + ((scr_u_done[c],) if qp else ()))
            tok_load = latest(tg, tu)
            last_gmm = last_umm = None
            for k in range(8):
                gb = k % 2
                ub = 2 + (k % 2)
                tok_gmm = None
                for i in range(8):
                    j = OCT_J[i][k]
                    rhs_t = lnT_t if OCT_G[i][k] > 0 else lnTn_t
                    for icc4 in range(4):
                        gic = i * 4 + icc4
                        wts = ()
                        if i == 0 and icc4 == 0:
                            wts = latest(tok_load, tok_lnT, tok_lnTn, evac_toks[gb])
                        tok_gmm = plan.step("tensor",
                                            (lambda e, gb=gb, j=j, icc4=icc4, gic=gic, rhs_t=rhs_t,
                                                    sl=sl, st=(i == 0 and icc4 == 0),
                                                    sp=(i == 7 and icc4 == 3):
                                             e.matmul(pb[gb][:, :],
                                                      gsl[sl][:, (j * 4 + icc4) * 128:(j * 4 + icc4 + 1) * 128],
                                                      rhs_t[:, gic * RT:(gic + 1) * RT],
                                                      start=st, stop=sp)),
                                            waits=wts, inc=sPE if (i == 7 and icc4 == 3) else None)
                tok_umm = None
                for i in range(8):
                    j = OCT_J[i][k]
                    rhs_t = lnT_t if OCT_G[i][k] > 0 else lnTn_t
                    for icc4 in range(4):
                        gic = i * 4 + icc4
                        wts = ()
                        if i == 0 and icc4 == 0:
                            wts = latest(evac_toks[ub])
                        tok_umm = plan.step("tensor",
                                            (lambda e, ub=ub, j=j, icc4=icc4, gic=gic, rhs_t=rhs_t,
                                                    sl=sl, st=(i == 0 and icc4 == 0),
                                                    sp=(i == 7 and icc4 == 3):
                                             e.matmul(pb[ub][:, :],
                                                      usl[sl][:, (j * 4 + icc4) * 128:(j * 4 + icc4 + 1) * 128],
                                                      rhs_t[:, gic * RT:(gic + 1) * RT],
                                                      start=st, stop=sp)),
                                            waits=wts, inc=sPE if (i == 7 and icc4 == 3) else None)
                last_gmm, last_umm = tok_gmm, tok_umm
                tok_si = plan.step("scalar",
                                   (lambda e, gb=gb: e.activation(out=silu_t[:, :], in_=pb[gb][:, :],
                                                                  func=AF.Silu,
                                                                  scale=scales[:, C_SG2:C_SG2 + 1])),
                                   waits=latest(tok_gmm, prev_tok_h,
                                                *(dreg_w if (c == 0 and k == 0) else ())),
                                   inc=sAC)
                evac_toks[gb] = tok_si
                kk = k % 4
                tok_h = plan.step("vector",
                                  (lambda e, ub=ub, kk=kk:
                                   e.scalar_tensor_tensor(out=hst[:, kk * RT:(kk + 1) * RT],
                                                          in0=pb[ub][:, :],
                                                          scalar=scales[:, C_SU2:C_SU2 + 1],
                                                          in1=silu_t[:, :],
                                                          op0=ALU.mult, op1=ALU.mult)),
                                  waits=latest(tok_umm, tok_si, *hfree2[kk // 2]), inc=sDV)
                evac_toks[ub] = tok_h
                prev_tok_h = tok_h
                half_toks.append(tok_h)
                if k % 2 == 1:
                    bk0 = k - 1
                    dst = h_scr.rearrange("(k c) p r -> c k p r", c=CB)[c, bk0:bk0 + 2] \
                               .rearrange("k p r -> p k r")
                    tok_hd = plan.dma("gpsimd", dst,
                                      hst[:, (bk0 % 4) * RT:(bk0 % 4 + 2) * RT]
                                      .rearrange("p (g r) -> p g r", g=2),
                                      stream=f"hw{(bk0 % 4) // 2}",
                                      waits=tuple(half_toks[-2:]))
                    hdma_toks.append(tok_hd)
                    hfree2[(bk0 % 4) // 2] = (tok_hd,)
                if qp:
                    pump_quant(3)
            gu_slot_free[sl] = (last_gmm, last_umm)
            pump_prep(1 if qp else 5)
        tok_h_all = latest(*hdma_toks)

        fT_toks = []
        dsl_free = ()
        for w in range(4):
            if qp:
                pump_quant_to(16 * CB + 8 * (w + 1))
            lw = tuple(dsl_free) + (tok_h_all,)
            td = plan.dma("sync", dsl[:, :], scr_d[w].rearrange("p a b c -> p (a b c)"),
                          stream="dsl", waits=lw + (tuple(first_wd) if w == 0 else ())
                          + ((scr_d_done[w],) if qp else ()))
            tn = plan.step("vector",
                           (lambda e, dsln_t=dsln_t: e.tensor_scalar(out=dsln_t[:, :], in0=dsl[:, :],
                                                                     scalar1=-1.0, scalar2=None,
                                                                     op0=ALU.mult)),
                           waits=(td,), inc=sDV)
            hload = [None, None]
            hfree = [(), ()]
            last_mm = None
            for g in range(CB):
                sl = g % 2
                if g < 2:
                    hload[sl] = plan.dma("sync", hrd[sl][:, :],
                                         h_scr[g * 8:(g + 1) * 8].rearrange("g p r -> p g r"),
                                         stream=f"hrd{sl}", waits=(tok_h_all,) + tuple(hfree[sl]))
                tok_g = None
                for k2 in range(8):
                    for hci in range(8):
                        hc = g * 8 + hci
                        i2, dc = divmod(hc, CB)
                        j = OCT_J[i2][k2]
                        src = dsl if OCT_G[i2][k2] > 0 else dsln_t
                        wts = ()
                        if k2 == 0 and hci == 0:
                            wts = latest(td, tn, hload[sl],
                                         *(fT_toks[-8:] if (w > 0 and g == 0) else ()))
                        tok_g = plan.step("tensor",
                                          (lambda e, k2=k2, j=j, dc=dc, src=src, hci=hci, sl=sl,
                                                  st=(g == 0 and hci == 0),
                                                  sp=(g == CB - 1 and hci == 7):
                                           e.matmul(pb[k2][:, :],
                                                    src[:, (j * CB + dc) * 128:(j * CB + dc + 1) * 128],
                                                    hrd[sl][:, hci * RT:(hci + 1) * RT],
                                                    start=st, stop=sp)),
                                          waits=wts, inc=sPE if (k2 == 7 and hci == 7) else None)
                last_mm = tok_g
                hfree[sl] = (tok_g,)
                if g + 2 <= CB - 1:
                    hload[sl] = plan.dma("sync", hrd[sl][:, :],
                                         h_scr[(g + 2) * 8:(g + 3) * 8].rearrange("g p r -> p g r"),
                                         stream=f"hrd{sl}", waits=tuple(hfree[sl]))
                if qp:
                    pump_quant(1)
            prev_ev2 = None
            for k2 in range(8):
                tok_e = plan.step("vector",
                                  (lambda e, k2=k2:
                                   e.tensor_scalar(out=dstag[:, :], in0=pb[k2][:, :],
                                                   scalar1=scales[:, C_SD4:C_SD4 + 1], scalar2=None,
                                                   op0=ALU.mult)),
                                  waits=latest(last_mm, *(fT_toks[-1:] if fT_toks else ())),
                                  inc=sDV)
                tok_tm = None
                for t4 in range(4):
                    tok_tm = plan.step("tensor",
                                       (lambda e, k2=k2, t4=t4:
                                        e.matmul(pbB[k2][:, t4 * 128:(t4 + 1) * 128],
                                                 dstag[:, t4 * 128:(t4 + 1) * 128],
                                                 ident[:, :], start=(t4 == 0), stop=(t4 == 3),
                                                 is_transpose=True)),
                                       waits=(tok_e,) if t4 == 0 else (),
                                       inc=sPE if t4 == 3 else None)
                tok_e2 = plan.step("vector",
                                   (lambda e, k2=k2: e.tensor_copy(out=fstag[:, :],
                                                                   in_=pbB[k2][:, 0:512])),
                                   waits=latest(tok_tm, prev_ev2), inc=sDV)
                prev_ev2 = tok_e2
                dmpos = k2 * 512 + w * 128
                tok_f = plan.dma("gpsimd",
                                 fT_scr.rearrange("(t p) d -> p t d", p=128)[:, :, dmpos:dmpos + 128],
                                 fstag[:, :].rearrange("p (t c) -> p t c", t=4),
                                 stream="ftw", waits=(tok_e2,))
                fT_toks.append(tok_f)
            dsl_free = (last_mm,)
            pump_prep(1 if qp else 4)
        if qp:
            pump_quant(Q_TOTAL)
        return latest(*fT_toks)

    tok_ln2_parts = []
    tok_x1_parts = []
    for rt in range(NRT):
        tok_T, _ = emit_ln_pass(rt, [("sync", x0)], None, lnT, 0, RT)
        if rt == 0:
            emit_scales_and_steps()
        tok_n = plan.step("vector",
                          (lambda e: e.tensor_scalar(out=lnTn[:, :], in0=lnT[:, :],
                                                     scalar1=-1.0, scalar2=None, op0=ALU.mult)),
                          waits=(tok_T,), inc=sDV)
        pump_prep(4)
        tok_fT = emit_ffn(tok_T, tok_n, (), qp=(rt == 0))
        tok_ln2, tok_x1 = emit_ln_pass(rt, [("sync", x0), ("scalar", fT_scr)],
                                       ("gpsimd", x1_scr, "x1w"), ln2T, rt * RT, RPC,
                                       extra_w=(tok_fT,) + ((scr_d_done[3],)
                                                            if rt == 0 else ()))
        tok_ln2_parts.append(tok_ln2)
        tok_x1_parts.append(tok_x1)
        pump_prep(4)
    tok_ln2_all = latest(*tok_ln2_parts)
    tok_x1_all = latest(*tok_x1_parts)

    pump_prep(len(prep_steps))
    tok_prep = prep_state["last"]

    wst_free = [(), ()]
    ctx_toks = []
    cs_guard = ()
    vth_guard = ()
    mix_end_guard = ()
    last_vth = {"t": None}
    pq_evg = {0: None, 1: None}
    pchain.setdefault("pb2", None)
    pchain.setdefault("pb3", None)
    pchain.setdefault("pb4", None)
    pchain.setdefault("pb5", None)

    for rt in range(NRT):
        vdma_toks = []
        qk_ev_toks = []
        vstg_guard = ()
        for m_i, (mT, bT_t, kind) in enumerate(((wvT, bvT, "v"), (wqT, bqT, "q"),
                                                (wkT, bkT, "k"))):
            for occ in range(ICC):
                wsl = occ % 2
                lw = tuple(wst_free[wsl])
                if rt == 0 and occ < 2:
                    lw = lw + (prep_state["done_" + kind],)
                    if m_i == 0:
                        lw = lw + (tok_ln2_all,)
                if kind == "q" and occ < 2:
                    lw = lw + tuple(mix_end_guard)
                tok_wl = plan.dma("sync", wst[wsl][:, :],
                                  mT.rearrange("(i p) o -> p i o", p=128)
                                  [:, :, occ * 128:(occ + 1) * 128],
                                  stream=f"wst{wsl}", waits=lw)
                bank = occ % 2
                tok_mm = None
                for icc in range(ICC):
                    wts = ()
                    if icc == 0:
                        wts = latest(tok_wl, tok_ln2_all, tok_bvT, pq_evg[bank])
                    tok_mm = plan.step("tensor",
                                       (lambda e, icc=icc, bank=bank, wsl=wsl, rt=rt:
                                        e.matmul(pb[bank][:, :],
                                                 wst[wsl][:, icc * 128:(icc + 1) * 128],
                                                 ln2T[:, :].rearrange("p (i r) -> p i r", r=RPC)
                                                 [:, icc, rt * RT:(rt + 1) * RT],
                                                 start=(icc == 0), stop=(icc == ICC - 1))),
                                       waits=wts, inc=sPE if icc == ICC - 1 else None)
                if kind == "q":
                    dstw = qTr[:, :].rearrange("p (o r) -> p o r", r=RT)[:, occ, :]
                elif kind == "k":
                    dstw = kTr[:, :].rearrange("p (o r) -> p o r", r=RT)[:, occ, :]
                else:
                    dstw = vstg[:, :]
                ev_w = (tok_mm,) + (vstg_guard if kind == "v" else ())
                if kind == "v" and occ == 0:
                    ev_w = ev_w + (tok_x1_all,)
                tok_ev = plan.step("vector",
                                   (lambda e, bank=bank, occ=occ, dstw=dstw, bT_t=bT_t:
                                    e.tensor_scalar(out=dstw, in0=pb[bank][:, :],
                                                    scalar1=bT_t[:, occ:occ + 1], scalar2=None,
                                                    op0=ALU.add)),
                                   waits=ev_w, inc=sDV)
                pq_evg[bank] = tok_ev
                if kind == "v":
                    tok_vd = plan.dma("gpsimd", vt_scr[occ * 128:(occ + 1) * 128, :],
                                      vstg[:, :], stream="vtw",
                                      waits=latest(tok_ev, *( (last_vth["t"],) if occ == 0 else () )))
                    vdma_toks.append(tok_vd)
                    vstg_guard = (tok_vd,)
                else:
                    qk_ev_toks.append(tok_ev)
                wst_free[wsl] = (tok_mm, tok_ev)
        tok_qk_all = latest(*qk_ev_toks)

        sc_war = {4: None, 5: None}
        vsb_war = {0: None, 1: None}
        cse_chain = {0: None, 1: None, 2: None, 3: None}
        prev_exp = None
        prev_pn = None
        prev_mm2 = None
        for half in range(2):
            tok_vh = plan.dma("sync",
                              vthalf[:, :].rearrange("p (o r) -> p o r", r=256),
                              vt_scr.rearrange("(o p) r -> p o r", p=128)
                              [:, :, half * 256:(half + 1) * 256],
                              stream="vth",
                              waits=latest(*vdma_toks, *vth_guard))
            for g in range(16):
                s0 = half * 256 + g * 16
                scb = 4 + g % 2
                vtb = 6 + g % 2
                tok_mm1 = None
                for i in range(16):
                    s_ = s0 + i
                    wts = ()
                    if i == 0:
                        wts = latest(tok_qk_all, sc_war[scb], pq_evg[0], pq_evg[1],
                                     pchain[f"pb{scb}"])
                    tok_mm1 = plan.step("tensor",
                                        (lambda e, i=i, s_=s_, scb=scb:
                                         e.matmul(pb[scb][0:32, i * 32:(i + 1) * 32],
                                                  qTr[:, :].rearrange("p (h s) -> p h s", s=RT)
                                                  [:, :, s_],
                                                  kTr[:, :].rearrange("p (h s) -> p h s", s=RT)
                                                  [:, :, s_],
                                                  start=(i == 0), stop=(i == 15))),
                                        waits=wts, inc=sPE if i == 15 else None)
                tok_vt_f = []
                tok_vsb0 = None
                for fill in range(2):
                    tk = None
                    for i8 in range(8):
                        i = fill * 8 + i8
                        s_ = s0 + i
                        wts = ()
                        if i == 0:
                            wts = latest(tok_vh, vsb_war[vtb - 6], pchain[f"pb{vtb}"])
                        elif i == 8:
                            wts = latest(tok_vsb0,)
                        tk = plan.step("tensor",
                                       (lambda e, i8=i8, s_=s_, vtb=vtb, half=half:
                                        e.matmul(pbB[vtb][0:32, i8 * 128:(i8 + 1) * 128],
                                                 vthalf[:, :].rearrange("p (h s) -> p h s", s=256)
                                                 [:, :, s_ - half * 256],
                                                 ident[:, :],
                                                 start=(i8 == 0), stop=(i8 == 7),
                                                 is_transpose=True)),
                                       waits=wts, inc=sPE if i8 == 7 else None)
                    tok_vt_f.append(tk)
                    if fill == 0:
                        tok_vsb0 = plan.step("vector",
                                             (lambda e, vtb=vtb:
                                              e.tensor_copy(out=vsb[0:32, 0:1024],
                                                            in_=pbB[vtb][0:32, 0:1024])),
                                             waits=latest(tk, prev_mm2), inc=sDV)
                tok_vsb1 = plan.step("vector",
                                     (lambda e, vtb=vtb:
                                      e.tensor_copy(out=vsb[0:32, 1024:2048],
                                                    in_=pbB[vtb][0:32, 0:1024])),
                                     waits=latest(tok_vt_f[1], prev_mm2), inc=sDV)
                vsb_war[vtb - 6] = tok_vsb1
                pchain[f"pb{vtb}"] = tok_vsb1
                tok_mx = plan.step("vector",
                                   (lambda e, scb=scb:
                                    e.reduce_max(out=mxs[0:32, :],
                                                 in_=pb[scb][0:32, :]
                                                 .rearrange("p (s t) -> p s t", t=32),
                                                 axis=AXX)),
                                   waits=(tok_mm1,), inc=sDV)
                tok_sub = plan.step("vector",
                                    (lambda e, scb=scb:
                                     e.tensor_tensor(out=sub_sb[0:32, :]
                                                     .rearrange("p (s t) -> p s t", t=32),
                                                     in0=pb[scb][0:32, :]
                                                     .rearrange("p (s t) -> p s t", t=32),
                                                     in1=mxs[0:32, :].unsqueeze(2)
                                                     .broadcast_to([32, 16, 32]),
                                                     op=ALU.subtract)),
                                    waits=latest(tok_mx, prev_exp), inc=sDV)
                sc_war[scb] = tok_sub
                pchain[f"pb{scb}"] = tok_sub
                tok_exp = plan.step("scalar",
                                    (lambda e: e.activation(out=prob[0:32, :],
                                                            in_=sub_sb[0:32, :], func=AF.Exp,
                                                            scale=float(SQD))),
                                    waits=latest(tok_sub, prev_pn), inc=sAC)
                prev_exp = tok_exp
                tok_sum = plan.step("vector",
                                    (lambda e: e.reduce_sum(out=dens[0:32, :],
                                                            in_=prob[0:32, :]
                                                            .rearrange("p (s t) -> p s t", t=32),
                                                            axis=AXX)),
                                    waits=(tok_exp,), inc=sDV)
                tok_rec = plan.step("vector",
                                    lambda e: e.reciprocal(out=recs[0:32, :], in_=dens[0:32, :]),
                                    waits=(tok_sum,), inc=sDV)
                tok_pn = plan.step("vector",
                                   (lambda e: e.tensor_tensor(out=pn[0:32, :]
                                                              .rearrange("p (s t) -> p s t", t=32),
                                                              in0=prob[0:32, :]
                                                              .rearrange("p (s t) -> p s t", t=32),
                                                              in1=recs[0:32, :].unsqueeze(2)
                                                              .broadcast_to([32, 16, 32]),
                                                              op=ALU.mult)),
                                   waits=(tok_rec,), inc=sDV)
                prev_pn = tok_pn
                tok_pt = plan.step("vector",
                                   lambda e: e.transpose(out=pt[0:32, :], in_=pn[0:32, :]),
                                   waits=latest(tok_pn, prev_mm2), inc=sDV)
                mm2_toks = []
                cse_now = []
                slotbase = (g % 2) * 16
                for p4 in range(4):
                    ctb = 2 + (p4 % 2)
                    tk = None
                    for i4 in range(4):
                        i = p4 * 4 + i4
                        wts = ()
                        if i4 == 0:
                            wts = latest(tok_pt, tok_vsb1, cse_chain[p4], cse_chain[p4 ^ 2])
                        tk = plan.step("tensor",
                                       (lambda e, i=i, i4=i4, ctb=ctb:
                                        e.matmul(pb[ctb][0:32, i4 * 128:(i4 + 1) * 128],
                                                 pt[0:32, i * 32:(i + 1) * 32],
                                                 vsb[0:32, i * 128:(i + 1) * 128],
                                                 start=(i4 == 0), stop=(i4 == 3))),
                                       waits=wts, inc=sPE if i4 == 3 else None)
                    mm2_toks.append(tk)
                    tok_cse = plan.step("scalar",
                                        (lambda e, p4=p4, slotbase=slotbase, ctb=ctb:
                                         e.copy(out=cs[0:32,
                                                       (slotbase + p4 * 4) * 128:
                                                       (slotbase + p4 * 4 + 4) * 128],
                                                in_=pb[ctb][0:32, :])),
                                        waits=latest(tk, *cs_guard), inc=sAC)
                    cse_chain[p4] = tok_cse
                    cse_now.append(tok_cse)
                prev_mm2 = latest(*mm2_toks)
                if g % 2 == 1:
                    rowbase = rt * RT + half * 256 + (g - 1) * 16
                    tok_cd = plan.dma("gpsimd",
                                      ctx_scr[rowbase:rowbase + 32, :]
                                      .rearrange("t (h d) -> h t d", d=128),
                                      cs[0:32, :].rearrange("p (t d) -> p t d", d=128),
                                      stream="cxw",
                                      waits=latest(*cse_now, cse_chain[0], cse_chain[1],
                                                   cse_chain[2], cse_chain[3]))
                    cs_guard = (tok_cd,)
                    ctx_toks.append(tok_cd)
            vth_guard = (tok_vt_f[1],)
        mix_end_guard = latest(sc_war[4], sc_war[5], cse_chain[0], cse_chain[1],
                               cse_chain[2], cse_chain[3], prev_mm2)
        last_vth["t"] = tok_vh
    pchain["pb2"] = latest(cse_chain[0], cse_chain[2])
    pchain["pb3"] = latest(cse_chain[1], cse_chain[3])
    tok_ctx_all = latest(*ctx_toks)

    out_toks = []
    tok_x2_all = None
    for rt in range(NRT):
        tok_T, tok_x2 = emit_ln_pass(rt, [("sync", x1_scr), ("scalar", ctx_scr)],
                                     ("gpsimd", x2_scr, "x2w"), lnT2, 0, RT,
                                     extra_w=(tok_x1_all, tok_ctx_all))
        tok_x2_all = tok_x2
        tok_n = plan.step("vector",
                          (lambda e: e.tensor_scalar(out=lnTn2[:, :], in0=lnT2[:, :],
                                                     scalar1=-1.0, scalar2=None, op0=ALU.mult)),
                          waits=(tok_T,), inc=sDV)
        tok_fT = emit_ffn(tok_T, tok_n, (), dreg_w=(tok_x2,),
                          lnT_t=lnT2, lnTn_t=lnTn2, dsln_t=dsln2)
        for sub in range(SUBRT):
            s = rt * SUBRT + sub
            sl = 0
            t1 = plan.dma("sync", xsub[sl][:, :], x2_scr[s * 128:(s + 1) * 128, :],
                          stream="xs", waits=tuple(xguard[sl]) + ((tok_x2, tok_fT) if sub == 0 else ()))
            xguard[sl] = ()
            t2 = plan.dma("scalar", ftsub[:, :], fT_scr[sub * 128:(sub + 1) * 128, :],
                          stream="fts", waits=tuple(ftguard["t"]) + ((tok_fT,) if sub == 0 else ()))
            t3 = plan.step("vector",
                           (lambda e, sl=sl: e.tensor_tensor(out=xsub[sl][:, :], in0=xsub[sl][:, :],
                                                             in1=ftsub[:, :], op=ALU.add)),
                           waits=(t1, t2), inc=sDV)
            ftguard["t"] = (t3,)
            t4 = plan.dma("gpsimd", out[s * 128:(s + 1) * 128, :], xsub[sl][:, :],
                          sem=sOUT, waits=(t3,))
            xguard[sl] = (t3, t4)
            out_toks.append(t4)

    if debug_taps:
        taps = [
            ("d_scr_g", scr_g), ("d_scr_u", scr_u), ("d_scr_d", scr_d),
            ("d_h", h_scr), ("d_fT", fT_scr),
            ("d_x1", x1_scr), ("d_x2", x2_scr), ("d_ctx", ctx_scr),
            ("d_vt", vt_scr),
        ]
        wall = (sOUT, sOUT.count)
        for nm, src in taps:
            dt = nc.dram_tensor(nm, list(src.shape), src.dtype, kind="ExternalOutput").ap()
            plan.dma("gpsimd", dt, src, sem=sOUT, waits=(wall,))
        for nm, t, w in (("d_scales", scales, 8), ("d_sval", sval, 16)):
            dt = nc.dram_tensor(nm, [t.shape[0], w], F32, kind="ExternalOutput").ap()
            plan.dma("gpsimd", dt, t[0:t.shape[0], 0:w], sem=sOUT, waits=(wall,))

    plan.step("gpsimd", None, waits=((sOUT, sOUT.count),))

    plan.realize()
    return nc


_cache = {}


def kernel(L0_attn_hidden_states, L0_attn_wq, L0_attn_bq, L0_attn_wk, L0_attn_bk,
           L0_attn_wv, L0_attn_bv, L0_attn_wl, L0_attn_bl,
           L0_ffn_w_gate, L0_ffn_w_up, L0_ffn_w_down):
    x = np.ascontiguousarray(np.asarray(L0_attn_hidden_states).reshape(8192, 4096))
    args = {
        "wq": np.ascontiguousarray(np.asarray(L0_attn_wq)),
        "wk": np.ascontiguousarray(np.asarray(L0_attn_wk)),
        "wv": np.ascontiguousarray(np.asarray(L0_attn_wv)),
        "bq": np.ascontiguousarray(np.asarray(L0_attn_bq)),
        "bk": np.ascontiguousarray(np.asarray(L0_attn_bk)),
        "bv": np.ascontiguousarray(np.asarray(L0_attn_bv)),
        "wg": np.ascontiguousarray(np.asarray(L0_ffn_w_gate)),
        "wu": np.ascontiguousarray(np.asarray(L0_ffn_w_up)),
        "wd": np.ascontiguousarray(np.asarray(L0_ffn_w_down)),
    }
    if "nc" not in _cache:
        _cache["nc"] = build_program()
    nc = _cache["nc"]
    in_maps = [dict(args, x0=x[c * RPC:(c + 1) * RPC]) for c in range(NCORES)]
    res = run_bass_kernel_spmd(nc, in_maps, core_ids=list(range(NCORES)))
    outs = [res.results[c]["out"] for c in range(NCORES)]
    return np.concatenate(outs, axis=0).reshape(4, 2048, 4096)


if __name__ == "__main__":
    import time
    t0 = time.time()
    nc = build_program()
    print("built ok in %.1fs; instructions: %d" % (time.time() - t0, nc.n_instructions()))


# version 09 (final)
# speedup vs baseline: 1.0549x; 1.0269x over previous
import math
import numpy as np
import ml_dtypes
import concourse.bass as bass
import concourse.mybir as mybir
from concourse.bass_utils import run_bass_kernel_spmd

F32 = mybir.dt.float32
BF16 = mybir.dt.bfloat16
AF = mybir.ActivationFunctionType
ALU = mybir.AluOpType
AXX = mybir.AxisListType.X

NCORES = 8
D = 4096
RPC = 1024
NSUB = RPC // 128
RT = 512
NRT = RPC // RT
SUBRT = RT // 128
ICC = D // 128
H8 = 1366
D8 = 512
KB = 8
CB = 11
HC = KB * CB
NHEAD = 32
HD = 128
SQD = 1.0 / math.sqrt(HD)
B0 = 20.0
EPS = 1e-5
NW_GU = 8 * H8 * D8


def _oct_tables():
    TRIPLES = [(1, 2, 3), (1, 4, 5), (1, 7, 6), (2, 4, 6), (2, 5, 7), (3, 4, 7), (3, 6, 5)]
    S = np.zeros((8, 8, 8), np.float32)
    S[0, 0, 0] = 1.0
    for i in range(1, 8):
        S[0, i, i] = 1.0
        S[i, 0, i] = 1.0
        S[i, i, 0] = -1.0
    for a, b, c in TRIPLES:
        for x, y, z in ((a, b, c), (b, c, a), (c, a, b)):
            S[x, y, z] = 1.0
            S[y, x, z] = -1.0
    J = [[None] * 8 for _ in range(8)]
    G = [[0.0] * 8 for _ in range(8)]
    for i in range(8):
        for k in range(8):
            js = [j for j in range(8) if S[j, i, k] != 0]
            assert len(js) == 1
            J[i][k] = js[0]
            G[i][k] = float(S[js[0], i, k])
    return J, G


OCT_J, OCT_G = _oct_tables()


class Sem:
    __slots__ = ("h", "count", "name")

    def __init__(self, nc, name):
        self.h = nc.alloc_semaphore(name)
        self.count = 0
        self.name = name


class Plan:
    ENGINES = ("sync", "scalar", "tensor", "vector", "gpsimd")

    def __init__(self, nc):
        self.nc = nc
        self.q = {e: [] for e in self.ENGINES}
        self._sems = {}

    def sem(self, name):
        if name not in self._sems:
            self._sems[name] = Sem(self.nc, name)
        return self._sems[name]

    def step(self, eng, fn, waits=(), inc=None, inc_amt=1):
        tok = None
        if inc is not None:
            inc.count += inc_amt
            tok = (inc, inc.count)
        ws = latest(*waits)
        self.q[eng].append((ws, fn, tok, inc_amt))
        return tok

    def dma(self, eng, out, in_, waits=(), sem=None, stream=None, **kw):
        if sem is None:
            assert stream is not None
            sem = self.sem("D_" + stream)
        return self.step(eng, lambda e: e.dma_start(out=out, in_=in_, **kw),
                         waits=waits, inc=sem, inc_amt=16)

    def realize(self):
        streams = self.q
        with self.nc.Block() as block:
            def mk(ename):
                def body(eng):
                    waited = {}
                    for ws, fn, tok, amt in streams[ename]:
                        for (s, v) in ws:
                            if waited.get(s.name, 0) >= v:
                                continue
                            eng.wait_ge(s.h, v)
                            waited[s.name] = v
                        ins = fn(eng) if fn is not None else None
                        if tok is not None:
                            (ins if ins is not None else eng.nop()).then_inc(tok[0].h, amt)
                return body
            block.sync(mk("sync"))
            block.scalar(mk("scalar"))
            block.tensor(mk("tensor"))
            block.vector(mk("vector"))
            block.gpsimd(mk("gpsimd"))


def latest(*toks):
    best = {}

    def add(t):
        if t is None:
            return
        if isinstance(t, tuple) and len(t) == 2 and isinstance(t[0], Sem):
            s, v = t
            if s.name not in best or best[s.name][1] < v:
                best[s.name] = (s, v)
        else:
            for x in t:
                add(x)

    for t in toks:
        add(t)
    return tuple(best.values())


def build_program(debug_taps=False):
    nc = bass.Bass("TRN2", target_bir_lowering=False, debug=False)

    x0 = nc.dram_tensor("x0", [RPC, D], F32, kind="ExternalInput").ap()
    wq = nc.dram_tensor("wq", [D, D], F32, kind="ExternalInput").ap()
    wk = nc.dram_tensor("wk", [D, D], F32, kind="ExternalInput").ap()
    wv = nc.dram_tensor("wv", [D, D], F32, kind="ExternalInput").ap()
    bq = nc.dram_tensor("bq", [D], F32, kind="ExternalInput").ap()
    bk = nc.dram_tensor("bk", [D], F32, kind="ExternalInput").ap()
    bv = nc.dram_tensor("bv", [D], F32, kind="ExternalInput").ap()
    wg = nc.dram_tensor("wg", [8, H8, D8], F32, kind="ExternalInput").ap()
    wu = nc.dram_tensor("wu", [8, H8, D8], F32, kind="ExternalInput").ap()
    wd = nc.dram_tensor("wd", [8, D8, H8], F32, kind="ExternalInput").ap()
    out = nc.dram_tensor("out", [RPC, D], F32, kind="ExternalOutput").ap()

    wqB = nc.dram_tensor("wqB", [D, D], BF16).ap()
    wkB = nc.dram_tensor("wkB", [D, D], BF16).ap()
    wvB = nc.dram_tensor("wvB", [D, D], BF16).ap()
    wqT = nc.dram_tensor("wqT", [D, D], BF16).ap()
    wkT = nc.dram_tensor("wkT", [D, D], BF16).ap()
    wvT = nc.dram_tensor("wvT", [D, D], BF16).ap()
    scr_g = nc.dram_tensor("scr_g", [CB, 128, KB, 4, 128], BF16).ap()
    scr_u = nc.dram_tensor("scr_u", [CB, 128, KB, 4, 128], BF16).ap()
    scr_d = nc.dram_tensor("scr_d", [4, 128, KB, CB, 128], BF16).ap()
    h_scr = nc.dram_tensor("h_scr", [HC, 128, RT], BF16).ap()
    fT_scr = nc.dram_tensor("fT_scr", [RT, D], BF16).ap()
    x1_scr = nc.dram_tensor("x1_scr", [RPC, D], F32).ap()
    x2_scr = nc.dram_tensor("x2_scr", [RPC, D], F32).ap()
    ctx_scr = nc.dram_tensor("ctx_scr", [RPC, D], BF16).ap()
    vt_scr = nc.dram_tensor("vt_scr", [D, RT], BF16).ap()

    identD = nc.inline_tensor(np.eye(128, dtype=ml_dtypes.bfloat16), name="identc").ap()
    ones1fD = nc.inline_tensor(np.ones((1, 128), np.float32), name="ones1fc").ap()
    ones1bD = nc.inline_tensor(np.ones((1, 128), ml_dtypes.bfloat16), name="ones1bc").ap()
    onescbD = nc.inline_tensor(np.ones((128, 1), ml_dtypes.bfloat16), name="onescbc").ap()
    onescfD = nc.inline_tensor(np.ones((128, 1), np.float32), name="onescfc").ap()
    constsD = nc.inline_tensor(
        np.tile(np.array([[0.5, -0.5, EPS, 0.0]], np.float32), (128, 1)), name="constsc").ap()

    ident = nc.alloc_sbuf_tensor("ident", [128, 128], BF16)
    ones1f = nc.alloc_sbuf_tensor("ones1f_s", [1, 128], F32)
    ones1b = nc.alloc_sbuf_tensor("ones1b_s", [1, 128], BF16)
    onescb = nc.alloc_sbuf_tensor("onescb_s", [128, 1], BF16)
    onescf = nc.alloc_sbuf_tensor("onescf_s", [128, 1], F32)
    consts = nc.alloc_sbuf_tensor("consts", [128, 4], F32)
    scales = nc.alloc_sbuf_tensor("scales", [128, 8], F32)
    C_RG, C_RU, C_RD, C_SG2, C_SU2, C_SD4, C_EB = range(7)
    sstat = nc.alloc_sbuf_tensor("sstat", [128, 8], F32)
    redq = nc.alloc_sbuf_tensor("redq", [128, 32], F32)
    sval = nc.alloc_sbuf_tensor("sval", [1, 16], F32)
    bqT = nc.alloc_sbuf_tensor("bqT", [128, ICC], F32)
    bkT = nc.alloc_sbuf_tensor("bkT", [128, ICC], F32)
    bvT = nc.alloc_sbuf_tensor("bvT", [128, ICC], F32)
    nrmio = nc.alloc_sbuf_tensor("nrmio", [1, 1], F32)

    offA, _ = nc.bump_sbuf(65536)
    offB, _ = nc.bump_sbuf(65536)
    offC, _ = nc.bump_sbuf(32768)
    offD, _ = nc.bump_sbuf(24576)
    offE, _ = nc.bump_sbuf(16384)

    def at(name, shape, dtype, base, off):
        return nc.alloc_sbuf_tensor_at(name, shape, dtype, offset=base + off)

    lnT = at("lnT", [128, ICC * RT], BF16, offA, 0)
    lnTn = at("lnTn", [128, ICC * RT], BF16, offA, 32768)
    qTr = at("qTr", [128, ICC * RT], BF16, offA, 0)
    kTr = at("kTr", [128, ICC * RT], BF16, offA, 32768)
    qsrc = [at(f"qsrc{i}", [128, 1408], F32, offB, i * 5632) for i in range(2)]
    qtq = at("qtq", [128, 1408], BF16, offB, 11264)
    qtq2 = at("qtq2", [128, 1408], BF16, offB, 14080)
    qstag = at("qstag", [128, 1408], BF16, offB, 16896)
    ln2T = at("ln2T", [128, ICC * RPC], BF16, offB, 0)
    lnT2 = at("lnT2", [128, ICC * RT], BF16, offB, 0)
    lnTn2 = at("lnTn2", [128, ICC * RT], BF16, offB, 32768)
    dsln2 = at("dsln2", [128, HC * 128], BF16, offB, 0)

    gsl = [at(f"gsl{i}", [128, 4096], BF16, offC, i * 8192) for i in range(2)]
    usl = [at(f"usl{i}", [128, 4096], BF16, offC, 16384 + i * 8192) for i in range(2)]
    dsl = at("dsl", [128, HC * 128], BF16, offC, 0)
    dsln = at("dsln", [128, HC * 128], BF16, offA, 0)
    sqs = at("sqs", [128, D], BF16, offC, 0)
    ftsub = at("ftsub", [128, D], BF16, offC, 8192)
    wst = [at(f"wst{i}", [128, ICC * 128], BF16, offC, i * 8192) for i in range(2)]
    wvcol = at("wvcol", [128, ICC * 256], BF16, offC, 16384)
    xsub = [at("xsub0", [128, D], F32, offD, 0)]
    hst = at("hst", [128, 4 * RT], BF16, offD, 0)
    silu_t = at("silu_t", [128, RT], F32, offD, 4096)
    dstag = at("dstag", [128, RT], BF16, offD, 6144)
    fstag = at("fstag", [128, 512], BF16, offD, 7168)
    hrd = [at(f"hrd{i}", [128, 8 * RT], BF16, offD, 8192 + i * 8192) for i in range(2)]

    vthalf = at("vthalf", [128, ICC * 256], BF16, offD, 0)
    cs = at("cs", [32, 32 * 128], BF16, offD, 16384)
    brow = at("brow", [1, D], F32, offE, 0)
    tstag = at("tstag", [128, D], BF16, offE, 0)
    sub_sb = at("sub_sb", [32, 512], F32, offE, 0)
    prob = at("prob", [32, 512], F32, offE, 2048)
    pn = at("pn", [32, 512], BF16, offE, 4096)
    pt = at("pt", [32, 512], BF16, offE, 5120)
    vsb = at("vsb", [32, 16 * 128], BF16, offE, 6144)
    mxs = at("mxs", [32, 16], F32, offE, 10240)
    vstg = at("vstg", [128, RT], BF16, offE, 11264)
    dens = at("dens", [32, 16], F32, offE, 10496)
    recs = at("recs", [32, 16], F32, offE, 10752)

    pb = [nc.alloc_psum_tensor(f"pb{i}", [128, 512], F32) for i in range(8)]
    pbB = [p.ap().bitcast(BF16) for p in pb]

    plan = Plan(nc)
    sIN = plan.sem("sIN")
    sSC = plan.sem("sSC")
    sGP = plan.sem("sGP")
    sPE = plan.sem("sPE")
    sDV = plan.sem("sDV")
    sAC = plan.sem("sAC")
    sCC = plan.sem("sCC")
    sOUT = plan.sem("sOUT")

    xguard = {0: (), 1: ()}
    ftguard = {"t": ()}

    cast_toks = {}
    cast_steps = []
    for name, (srcm, dstB) in {"q": (wq, wqB), "k": (wk, wkB), "v": (wv, wvB)}.items():
        for ch in range(16):
            def go_cast(name=name, srcm=srcm, dstB=dstB, ch=ch):
                cast_toks[name] = plan.dma(
                    "gpsimd", dstB[ch * 256:(ch + 1) * 256, :],
                    srcm[ch * 256:(ch + 1) * 256, :], stream="cast_" + name)
            cast_steps.append(go_cast)

    tcl = []
    for dst, src in ((ident, identD), (ones1f, ones1fD), (ones1b, ones1bD),
                     (onescb, onescbD), (onescf, onescfD), (consts, constsD)):
        sh = dst.shape
        tcl.append(plan.dma("sync", dst[0:sh[0], 0:sh[1]], src, stream="const"))
    tok_consts = latest(*tcl)

    def bias_transpose(bsrc, bdst, extra):
        tok_l = plan.dma("sync", brow[0:1, :], bsrc[None, :], stream="brow", waits=extra)
        tok_mm = None
        for ch in range(ICC):
            tok_mm = plan.step("tensor",
                               (lambda e, ch=ch: e.matmul(pb[6][:, ch:ch + 1],
                                                          brow[0:1, ch * 128:(ch + 1) * 128],
                                                          ones1f[0:1, 0:1],
                                                          start=(ch == 0), stop=(ch == ICC - 1))),
                               waits=(tok_l, tok_consts) if ch == 0 else (),
                               inc=sPE if ch == ICC - 1 else None)
        tok_ev = plan.step("vector",
                           (lambda e: e.tensor_copy(out=bdst[0:128, 0:ICC], in_=pb[6][:, 0:ICC])),
                           waits=(tok_mm,), inc=sDV)
        return tok_ev

    tok_bqT = bias_transpose(bq, bqT, ())
    tok_bkT = bias_transpose(bk, bkT, (tok_bqT,))
    tok_bvT = bias_transpose(bv, bvT, (tok_bkT,))

    tok_ms_qstag = plan.step("vector", lambda e: e.memset(qstag[:, :], 0.0), inc=sDV)

    qsc = [gsl[0].ap().bitcast(F32), usl[0].ap().bitcast(F32)]
    qsc_guard = [(), ()]

    def quant_scale(wsrc, nelem, col_recip, col_out, out_div):
        flat = wsrc.rearrange("a b c -> (a b c)").rearrange("(p n) -> p n", p=128)
        percol = nelem // 128
        tok_r = None
        ci = 0
        pos = 0
        while pos < percol:
            cw = min(2048, percol - pos)
            sl = ci % 2
            tok_l = plan.dma("sync", qsc[sl][:, 0:cw], flat[:, pos:pos + cw],
                             stream=f"qsc{sl}", waits=qsc_guard[sl])
            tok_r = plan.step("vector",
                              (lambda e, sl=sl, cw=cw, ci=ci:
                               e.tensor_reduce(out=redq[:, ci:ci + 1], in_=qsc[sl][:, 0:cw],
                                               axis=AXX, op=ALU.add, apply_absolute_value=True)),
                              waits=(tok_l,), inc=sDV)
            qsc_guard[sl] = (tok_r,)
            pos += cw
            ci += 1
        tok_r2 = plan.step("vector",
                           (lambda e, ci=ci: e.tensor_reduce(out=redq[:, 31:32], in_=redq[:, 0:ci],
                                                             axis=AXX, op=ALU.add)),
                           waits=(tok_r,), inc=sDV)
        tok_mm = plan.step("tensor",
                           (lambda e: e.matmul(pb[6][0:1, 256:257], redq[:, 31:32], onescf[:, 0:1],
                                               start=True, stop=True)),
                           waits=(tok_r2,), inc=sPE)
        tok_s = plan.step("vector",
                          lambda e: e.tensor_copy(out=sval[0:1, 0:1], in_=pb[6][0:1, 256:257]),
                          waits=(tok_mm,), inc=sDV)
        tok_rc = plan.step("vector", lambda e: e.reciprocal(out=sval[0:1, 1:2], in_=sval[0:1, 0:1]),
                           waits=(tok_s,), inc=sDV)
        tok_rs = plan.step("vector",
                           (lambda e: e.tensor_scalar(out=sval[0:1, 2:3], in0=sval[0:1, 1:2],
                                                      scalar1=float(nelem), scalar2=None, op0=ALU.mult)),
                           waits=(tok_rc,), inc=sDV)
        tok_os = plan.step("vector",
                           (lambda e: e.tensor_scalar(out=sval[0:1, 3:4], in0=sval[0:1, 0:1],
                                                      scalar1=float(1.0 / (nelem * out_div)),
                                                      scalar2=None, op0=ALU.mult)),
                           waits=(tok_s,), inc=sDV)
        tok_b1 = plan.step("tensor",
                           (lambda e: e.matmul(pb[6][:, 257:258], ones1f[0:1, :], sval[0:1, 2:3],
                                               start=True, stop=False)),
                           waits=(tok_rs, tok_os), inc=None)
        tok_b2 = plan.step("tensor",
                           (lambda e: e.matmul(pb[6][:, 258:259], ones1f[0:1, :], sval[0:1, 3:4],
                                               start=False, stop=True)),
                           waits=(), inc=sPE)
        tok_e1 = plan.step("vector",
                           (lambda e, col_recip=col_recip:
                            e.tensor_copy(out=scales[:, col_recip:col_recip + 1],
                                          in_=pb[6][:, 257:258])),
                           waits=(tok_b2,), inc=sDV)
        tok_e2 = plan.step("vector",
                           (lambda e, col_out=col_out:
                            e.tensor_copy(out=scales[:, col_out:col_out + 1],
                                          in_=pb[6][:, 258:259])),
                           waits=(tok_b2,), inc=sDV)
        return latest(tok_e1, tok_e2)

    pchain = {"pb6": None, "pb7": None}
    qchain = {"mm": None, "w": None}
    qslot = [(), ()]

    def quant_ternary(src_ap, rows, cols, col_recip, wts):
        tok_s1 = plan.step("scalar",
                           (lambda e: e.activation(out=qtq[0:rows, 0:cols], in_=src_ap,
                                                   func=AF.Sign, bias=consts[0:rows, 0:1],
                                                   scale=scales[0:rows, col_recip:col_recip + 1])),
                           waits=wts, inc=sAC)
        tok_s2 = plan.step("scalar",
                           (lambda e: e.activation(out=qtq2[0:rows, 0:cols], in_=src_ap,
                                                   func=AF.Sign, bias=consts[0:rows, 1:2],
                                                   scale=scales[0:rows, col_recip:col_recip + 1])),
                           waits=(tok_s1,), inc=sAC)
        tok_t = plan.step("vector",
                          (lambda e: e.tensor_tensor(out=qtq[0:rows, 0:cols],
                                                     in0=qtq[0:rows, 0:cols],
                                                     in1=qtq2[0:rows, 0:cols], op=ALU.add)),
                          waits=(tok_s2,), inc=sDV)
        return tok_t

    def quant_pass2_gu(wsrc, scr, col_recip, tok_scale):
        slot_free = qslot
        for j in range(8):
            for oct_ in range(CB):
                rows = 128 if oct_ < 10 else H8 - 1280
                sl = (j * CB + oct_) % 2
                tok_l = plan.dma("sync", qsrc[sl][0:rows, 0:D8],
                                 wsrc[j, oct_ * 128:oct_ * 128 + rows, :],
                                 stream=f"qs{sl}", waits=slot_free[sl])
                tok_t = quant_ternary(qsrc[sl][0:rows, 0:D8], rows, D8, col_recip,
                                      (tok_l, tok_scale, tok_ms_qstag, qchain["mm"]))
                slot_free[sl] = (tok_t,)
                tok_mm = None
                for icc4 in range(4):
                    tok_mm = plan.step("tensor",
                                       (lambda e, icc4=icc4, rows=rows:
                                        e.matmul(pbB[7][:, icc4 * 128:icc4 * 128 + rows],
                                                 qtq[0:rows, icc4 * 128:(icc4 + 1) * 128],
                                                 ident[0:rows, 0:rows],
                                                 start=(icc4 == 0), stop=(icc4 == 3),
                                                 is_transpose=True)),
                                       waits=latest(tok_t, pchain["pb7"]) if icc4 == 0 else (),
                                       inc=sPE if icc4 == 3 else None)
                qchain["mm"] = tok_mm
                tok_pad = None
                if rows < 128:
                    tok_pad = plan.step("vector",
                                        (lambda e, rows=rows:
                                         e.memset(qstag[:, 0:512]
                                                  .rearrange("p (i c) -> p i c", i=4)[:, :, rows:128],
                                                  0.0)),
                                        waits=latest(qchain["w"],), inc=sDV)
                tok_ev = plan.step("vector",
                                   (lambda e, rows=rows:
                                    e.tensor_copy(
                                        out=qstag[:, 0:512].rearrange("p (i c) -> p i c", i=4)[:, :, 0:rows],
                                        in_=pbB[7][:, 0:512].rearrange("p (i c) -> p i c", i=4)[:, :, 0:rows])),
                                   waits=latest(tok_mm, qchain["w"], tok_pad), inc=sDV)
                pchain["pb7"] = tok_ev
                tok_w = plan.dma("gpsimd", scr[oct_, :, j, :, :],
                                 qstag[:, 0:512].rearrange("p (i c) -> p i c", i=4),
                                 stream="wscr", waits=latest(tok_ev, tok_pad))
                qchain["w"] = tok_w
        return tok_w

    def quant_pass2_d(tok_scale):
        slot_free = qslot
        for j in range(8):
            for w in range(4):
                sl = (j * 4 + w) % 2
                tok_l = plan.dma("sync", qsrc[sl][:, 0:H8], wd[j, w * 128:(w + 1) * 128, :],
                                 stream=f"qs{sl}", waits=slot_free[sl])
                tok_t = quant_ternary(qsrc[sl][:, 0:H8], 128, H8, C_RD,
                                      (tok_l, tok_scale, tok_ms_qstag, qchain["mm"]))
                slot_free[sl] = (tok_t,)
                tok7 = tok6 = None
                for dc in list(range(8)) + [8, 10, 9]:
                    dw = 128 if dc < 10 else H8 - 1280
                    if dc < 8:
                        bank, pos = pbB[7], dc
                        first, lastc = dc == 0, dc == 7
                        chain = pchain["pb7"]
                    else:
                        bank, pos = pbB[6], dc - 8
                        first, lastc = dc == 8, dc == 9
                        chain = pchain["pb6"]
                    wts = latest(tok_t, chain) if first else ()
                    tk = plan.step("tensor",
                                   (lambda e, dc=dc, dw=dw, bank=bank, pos=pos,
                                           first=first, lastc=lastc:
                                    e.matmul(bank[0:dw, pos * 128:(pos + 1) * 128],
                                             qtq[:, dc * 128:dc * 128 + dw],
                                             ident[:, :],
                                             start=first, stop=lastc, is_transpose=True)),
                                   waits=wts, inc=sPE if lastc else None)
                    if dc == 7:
                        tok7 = tk
                    if dc == 9:
                        tok6 = tk
                qchain["mm"] = latest(tok7, tok6)
                tok_e1 = plan.step("vector",
                                   lambda e: e.tensor_copy(out=qstag[:, 0:1024], in_=pbB[7][:, 0:1024]),
                                   waits=latest(tok7, qchain["w"]), inc=sDV)
                tok_e2 = plan.step("vector",
                                   lambda e: e.tensor_copy(out=qstag[:, 1024:1280], in_=pbB[6][:, 0:256]),
                                   waits=(tok6,), inc=sDV)
                tok_e3 = plan.step("vector",
                                   (lambda e: e.tensor_copy(out=qstag[0:H8 - 1280, 1280:1408],
                                                            in_=pbB[6][0:H8 - 1280, 256:384])),
                                   waits=(tok6,), inc=sDV)
                pchain["pb7"] = tok_e1
                pchain["pb6"] = tok_e3
                tok_w = plan.dma("gpsimd", scr_d[w, :, j, :, :],
                                 qstag[:, 0:CB * 128].rearrange("p (i c) -> p i c", i=CB),
                                 stream="wscr", waits=(tok_e1, tok_e2, tok_e3))
                qchain["w"] = tok_w
        return tok_w

    tok_sg = quant_scale(wg, NW_GU, C_RG, C_SG2, 2.0)
    tok_su = quant_scale(wu, NW_GU, C_RU, C_SU2, 2.0)
    tok_sd = quant_scale(wd, NW_GU, C_RD, C_SD4, 4.0)
    tok_scr_g = quant_pass2_gu(wg, scr_g, C_RG, tok_sg)
    tok_scr_u = quant_pass2_gu(wu, scr_u, C_RU, tok_su)
    tok_scr_d = quant_pass2_d(tok_sd)
    tok_quant_all = latest(tok_scr_g, tok_scr_u, tok_scr_d)

    prep_steps = list(cast_steps)
    prep_state = {"tstag_free": (tok_bvT,), "last": None}

    for (mB, mT, ckey) in ((wqB, wqT, "q"), (wkB, wkT, "k"), (wvB, wvT, "v")):
        for icc in range(ICC):
            def go(mB=mB, mT=mT, ckey=ckey, icc=icc):
                t1 = plan.dma("scalar", tstag[:, :], mB[:, icc * 128:(icc + 1) * 128],
                              stream="prin", waits=(cast_toks[ckey],) + tuple(prep_state["tstag_free"]),
                              transpose=True)
                t2 = plan.dma("scalar", mT[icc * 128:(icc + 1) * 128, :], tstag[:, :],
                              stream="prout", waits=(t1,))
                prep_state["tstag_free"] = (t2,)
                prep_state["last"] = t2
            prep_steps.append(go)

    def pump_prep(n):
        for _ in range(n):
            if prep_steps:
                prep_steps.pop(0)()

    lnpass = {"evac6": None, "evac7": None, "sqs_guard": None}

    def emit_ln_pass(rt, srcs, spill_to, T_dst, T_rowbase, T_rowspan, extra_w=()):
        tok_T_all = []
        tok_spill = []
        for sub in range(SUBRT):
            s = rt * SUBRT + sub
            sl = 0
            q0, ap0 = srcs[0]
            tok_l0 = plan.dma(q0, xsub[sl][:, :], ap0[s * 128:(s + 1) * 128, :],
                              stream="xs", waits=tuple(xguard[sl]) + tuple(extra_w))
            xguard[sl] = ()
            tok_add = tok_l0
            if len(srcs) > 1:
                q1, ap1 = srcs[1]
                assert ap1.dtype == BF16
                dst1, g1 = ftsub, None
                gw = tuple(ftguard["t"])
                r1 = sub if ap1.shape[0] == RT else s
                tok_l1 = plan.dma(q1, dst1[:, :], ap1[r1 * 128:(r1 + 1) * 128, :],
                                  stream="fts", waits=gw + tuple(extra_w))
                tok_add = plan.step("vector",
                                    (lambda e, sl=sl, dst1=dst1:
                                     e.tensor_tensor(out=xsub[sl][:, :], in0=xsub[sl][:, :],
                                                     in1=dst1[:, :], op=ALU.add)),
                                    waits=(tok_l0, tok_l1), inc=sDV)
                if g1 is not None:
                    xguard[g1] = (tok_add,)
                else:
                    ftguard["t"] = (tok_add,)
            if spill_to is not None:
                qs, aps, spill_stream = spill_to
                tok_sp = plan.dma(qs, aps[s * 128:(s + 1) * 128, :], xsub[sl][:, :],
                                  stream=spill_stream, waits=(tok_add,))
                tok_spill.append(tok_sp)
            tok_sum = plan.step("vector",
                                (lambda e, sl=sl: e.reduce_sum(out=sstat[:, 0:1],
                                                               in_=xsub[sl][:, :], axis=AXX)),
                                waits=(tok_add,), inc=sDV)
            tok_sq = plan.step("scalar",
                               (lambda e, sl=sl: e.activation(out=sqs[:, :], in_=xsub[sl][:, :],
                                                              func=AF.Square,
                                                              accum_out=sstat[:, 1:2])),
                               waits=latest(tok_add, lnpass["sqs_guard"]), inc=sAC)
            tok_m = plan.step("vector",
                              (lambda e: e.tensor_scalar(out=sstat[:, 2:3], in0=sstat[:, 0:1],
                                                         scalar1=float(-1.0 / D), scalar2=None,
                                                         op0=ALU.mult)),
                              waits=(tok_sum,), inc=sDV)
            tok_m2 = plan.step("scalar",
                               (lambda e: e.activation(out=sstat[:, 3:4], in_=sstat[:, 2:3],
                                                       func=AF.Square)),
                               waits=(tok_m,), inc=sAC)
            tok_v = plan.step("vector",
                              (lambda e: e.scalar_tensor_tensor(out=sstat[:, 4:5], in0=sstat[:, 1:2],
                                                                scalar=float(1.0 / D),
                                                                in1=sstat[:, 3:4],
                                                                op0=ALU.mult, op1=ALU.subtract)),
                              waits=(tok_sq, tok_m2), inc=sDV)
            tok_lnv = plan.step("scalar",
                                (lambda e: e.activation(out=sstat[:, 5:6], in_=sstat[:, 4:5],
                                                        func=AF.Ln, bias=consts[:, 2:3])),
                                waits=(tok_v,), inc=sAC)
            tok_rstd = plan.step("scalar",
                                 (lambda e: e.activation(out=sstat[:, 6:7], in_=sstat[:, 5:6],
                                                         func=AF.Exp, scale=-0.5)),
                                 waits=(tok_lnv,), inc=sAC)
            tok_ap = plan.step("vector",
                               (lambda e, sl=sl: e.tensor_scalar(out=sqs[:, :], in0=xsub[sl][:, :],
                                                                 scalar1=sstat[:, 2:3],
                                                                 scalar2=sstat[:, 6:7],
                                                                 op0=ALU.add, op1=ALU.mult)),
                               waits=(tok_rstd, tok_sq), inc=sDV)
            xguard[sl] = xguard[sl] + (tok_ap,) + tuple(tok_spill[-1:] if spill_to else ())
            for pk in range(4):
                bank = 6 + (pk % 2)
                tok_mm = None
                for i8 in range(8):
                    icc = pk * 8 + i8
                    wts = latest(tok_ap, lnpass[f"evac{bank}"], pchain[f"pb{bank}"]) if i8 == 0 else ()
                    tok_mm = plan.step("tensor",
                                       (lambda e, bank=bank, i8=i8, icc=icc:
                                        e.matmul(pbB[bank][:, i8 * 128:(i8 + 1) * 128],
                                                 sqs[:, icc * 128:(icc + 1) * 128],
                                                 ident[:, :], start=(i8 == 0), stop=(i8 == 7),
                                                 is_transpose=True)),
                                       waits=wts, inc=sPE if i8 == 7 else None)
                tok_ev = plan.step("vector",
                                   (lambda e, bank=bank, pk=pk, sub=sub:
                                    e.tensor_copy(
                                        out=T_dst[:, :].rearrange("p (i r) -> p i r", r=T_rowspan)
                                        [:, pk * 8:(pk + 1) * 8,
                                         T_rowbase + sub * 128:T_rowbase + (sub + 1) * 128],
                                        in_=pbB[bank][:, 0:1024].rearrange("p (i r) -> p i r", r=128))),
                                   waits=(tok_mm,), inc=sDV)
                lnpass[f"evac{bank}"] = tok_ev
                pchain[f"pb{bank}"] = tok_ev
                tok_T_all.append(tok_ev)
            lnpass["sqs_guard"] = latest(*tok_T_all[-4:])
        return latest(*tok_T_all), (latest(*tok_spill) if tok_spill else None)

    def emit_ffn(tok_lnT, tok_lnTn, first_w, first_wu=(), first_wd=(), dreg_w=(),
                 lnT_t=None, lnTn_t=None, dsln_t=None):
        lnT_t = lnT_t if lnT_t is not None else lnT
        lnTn_t = lnTn_t if lnTn_t is not None else lnTn
        dsln_t = dsln_t if dsln_t is not None else dsln
        """first_w: extra waits for the first weight-slice loads of this call."""
        evac_toks = {0: None, 1: None, 2: None, 3: None}
        gu_slot_free = [(), ()]
        hfree2 = {0: (), 1: ()}
        hdma_toks = []
        half_toks = []
        prev_tok_h = None
        for c in range(CB):
            sl = c % 2
            lw = tuple(gu_slot_free[sl])
            if c < 2:
                lw = lw + tuple(first_w) + (tok_lnT,)
            tg = plan.dma("sync", gsl[sl][:, :],
                          scr_g[c].rearrange("p a b c -> p (a b c)"), stream=f"gsl{sl}", waits=lw)
            tu = plan.dma("sync", usl[sl][:, :],
                          scr_u[c].rearrange("p a b c -> p (a b c)"), stream=f"usl{sl}",
                          waits=tuple(gu_slot_free[sl]) + (tuple(first_wu) if c < 2 else ()))
            tok_load = latest(tg, tu)
            last_gmm = last_umm = None
            for k in range(8):
                gb = k % 2
                ub = 2 + (k % 2)
                tok_gmm = None
                for i in range(8):
                    j = OCT_J[i][k]
                    rhs_t = lnT_t if OCT_G[i][k] > 0 else lnTn_t
                    for icc4 in range(4):
                        gic = i * 4 + icc4
                        wts = ()
                        if i == 0 and icc4 == 0:
                            wts = latest(tok_load, tok_lnT, tok_lnTn, evac_toks[gb])
                        tok_gmm = plan.step("tensor",
                                            (lambda e, gb=gb, j=j, icc4=icc4, gic=gic, rhs_t=rhs_t,
                                                    sl=sl, st=(i == 0 and icc4 == 0),
                                                    sp=(i == 7 and icc4 == 3):
                                             e.matmul(pb[gb][:, :],
                                                      gsl[sl][:, (j * 4 + icc4) * 128:(j * 4 + icc4 + 1) * 128],
                                                      rhs_t[:, gic * RT:(gic + 1) * RT],
                                                      start=st, stop=sp)),
                                            waits=wts, inc=sPE if (i == 7 and icc4 == 3) else None)
                tok_umm = None
                for i in range(8):
                    j = OCT_J[i][k]
                    rhs_t = lnT_t if OCT_G[i][k] > 0 else lnTn_t
                    for icc4 in range(4):
                        gic = i * 4 + icc4
                        wts = ()
                        if i == 0 and icc4 == 0:
                            wts = latest(evac_toks[ub])
                        tok_umm = plan.step("tensor",
                                            (lambda e, ub=ub, j=j, icc4=icc4, gic=gic, rhs_t=rhs_t,
                                                    sl=sl, st=(i == 0 and icc4 == 0),
                                                    sp=(i == 7 and icc4 == 3):
                                             e.matmul(pb[ub][:, :],
                                                      usl[sl][:, (j * 4 + icc4) * 128:(j * 4 + icc4 + 1) * 128],
                                                      rhs_t[:, gic * RT:(gic + 1) * RT],
                                                      start=st, stop=sp)),
                                            waits=wts, inc=sPE if (i == 7 and icc4 == 3) else None)
                last_gmm, last_umm = tok_gmm, tok_umm
                tok_si = plan.step("scalar",
                                   (lambda e, gb=gb: e.activation(out=silu_t[:, :], in_=pb[gb][:, :],
                                                                  func=AF.Silu,
                                                                  scale=scales[:, C_SG2:C_SG2 + 1])),
                                   waits=latest(tok_gmm, prev_tok_h,
                                                *(dreg_w if (c == 0 and k == 0) else ())),
                                   inc=sAC)
                evac_toks[gb] = tok_si
                kk = k % 4
                tok_h = plan.step("vector",
                                  (lambda e, ub=ub, kk=kk:
                                   e.scalar_tensor_tensor(out=hst[:, kk * RT:(kk + 1) * RT],
                                                          in0=pb[ub][:, :],
                                                          scalar=scales[:, C_SU2:C_SU2 + 1],
                                                          in1=silu_t[:, :],
                                                          op0=ALU.mult, op1=ALU.mult)),
                                  waits=latest(tok_umm, tok_si, *hfree2[kk // 2]), inc=sDV)
                evac_toks[ub] = tok_h
                prev_tok_h = tok_h
                half_toks.append(tok_h)
                if k % 2 == 1:
                    bk0 = k - 1
                    dst = h_scr.rearrange("(k c) p r -> c k p r", c=CB)[c, bk0:bk0 + 2] \
                               .rearrange("k p r -> p k r")
                    tok_hd = plan.dma("gpsimd", dst,
                                      hst[:, (bk0 % 4) * RT:(bk0 % 4 + 2) * RT]
                                      .rearrange("p (g r) -> p g r", g=2),
                                      stream=f"hw{(bk0 % 4) // 2}",
                                      waits=tuple(half_toks[-2:]))
                    hdma_toks.append(tok_hd)
                    hfree2[(bk0 % 4) // 2] = (tok_hd,)
            gu_slot_free[sl] = (last_gmm, last_umm)
            pump_prep(4)
        tok_h_all = latest(*hdma_toks)

        fT_toks = []
        dsl_free = ()
        for w in range(4):
            lw = tuple(dsl_free) + (tok_h_all,)
            td = plan.dma("sync", dsl[:, :], scr_d[w].rearrange("p a b c -> p (a b c)"),
                          stream="dsl", waits=lw + (tuple(first_wd) if w == 0 else ()))
            tn = plan.step("vector",
                           (lambda e, dsln_t=dsln_t: e.tensor_scalar(out=dsln_t[:, :], in0=dsl[:, :],
                                                                     scalar1=-1.0, scalar2=None,
                                                                     op0=ALU.mult)),
                           waits=(td,), inc=sDV)
            hload = [None, None]
            hfree = [(), ()]
            last_mm = None
            for g in range(CB):
                sl = g % 2
                if g < 2:
                    hload[sl] = plan.dma("sync", hrd[sl][:, :],
                                         h_scr[g * 8:(g + 1) * 8].rearrange("g p r -> p g r"),
                                         stream=f"hrd{sl}", waits=(tok_h_all,) + tuple(hfree[sl]))
                tok_g = None
                for k2 in range(8):
                    for hci in range(8):
                        hc = g * 8 + hci
                        i2, dc = divmod(hc, CB)
                        j = OCT_J[i2][k2]
                        src = dsl if OCT_G[i2][k2] > 0 else dsln_t
                        wts = ()
                        if k2 == 0 and hci == 0:
                            wts = latest(td, tn, hload[sl],
                                         *(fT_toks[-8:] if (w > 0 and g == 0) else ()))
                        tok_g = plan.step("tensor",
                                          (lambda e, k2=k2, j=j, dc=dc, src=src, hci=hci, sl=sl,
                                                  st=(g == 0 and hci == 0),
                                                  sp=(g == CB - 1 and hci == 7):
                                           e.matmul(pb[k2][:, :],
                                                    src[:, (j * CB + dc) * 128:(j * CB + dc + 1) * 128],
                                                    hrd[sl][:, hci * RT:(hci + 1) * RT],
                                                    start=st, stop=sp)),
                                          waits=wts, inc=sPE if (k2 == 7 and hci == 7) else None)
                last_mm = tok_g
                hfree[sl] = (tok_g,)
                if g + 2 <= CB - 1:
                    hload[sl] = plan.dma("sync", hrd[sl][:, :],
                                         h_scr[(g + 2) * 8:(g + 3) * 8].rearrange("g p r -> p g r"),
                                         stream=f"hrd{sl}", waits=tuple(hfree[sl]))
            prev_ev2 = None
            for k2 in range(8):
                tok_e = plan.step("vector",
                                  (lambda e, k2=k2:
                                   e.tensor_scalar(out=dstag[:, :], in0=pb[k2][:, :],
                                                   scalar1=scales[:, C_SD4:C_SD4 + 1], scalar2=None,
                                                   op0=ALU.mult)),
                                  waits=latest(last_mm, *(fT_toks[-1:] if fT_toks else ())),
                                  inc=sDV)
                tok_tm = None
                for t4 in range(4):
                    tok_tm = plan.step("tensor",
                                       (lambda e, k2=k2, t4=t4:
                                        e.matmul(pbB[k2][:, t4 * 128:(t4 + 1) * 128],
                                                 dstag[:, t4 * 128:(t4 + 1) * 128],
                                                 ident[:, :], start=(t4 == 0), stop=(t4 == 3),
                                                 is_transpose=True)),
                                       waits=(tok_e,) if t4 == 0 else (),
                                       inc=sPE if t4 == 3 else None)
                tok_e2 = plan.step("vector",
                                   (lambda e, k2=k2: e.tensor_copy(out=fstag[:, :],
                                                                   in_=pbB[k2][:, 0:512])),
                                   waits=latest(tok_tm, prev_ev2), inc=sDV)
                prev_ev2 = tok_e2
                dmpos = k2 * 512 + w * 128
                tok_f = plan.dma("gpsimd",
                                 fT_scr.rearrange("(t p) d -> p t d", p=128)[:, :, dmpos:dmpos + 128],
                                 fstag[:, :].rearrange("p (t c) -> p t c", t=4),
                                 stream="ftw", waits=(tok_e2,))
                fT_toks.append(tok_f)
            dsl_free = (last_mm,)
            pump_prep(4)
        return latest(*fT_toks)

    tok_ln2_parts = []
    tok_x1_parts = []
    for rt in range(NRT):
        tok_T, _ = emit_ln_pass(rt, [("sync", x0)], None, lnT, 0, RT)
        tok_n = plan.step("vector",
                          (lambda e: e.tensor_scalar(out=lnTn[:, :], in0=lnT[:, :],
                                                     scalar1=-1.0, scalar2=None, op0=ALU.mult)),
                          waits=(tok_T,), inc=sDV)
        pump_prep(4)
        tok_fT = emit_ffn(tok_T, tok_n,
                          (tok_scr_g,) if rt == 0 else (),
                          first_wu=(tok_scr_u,) if rt == 0 else (),
                          first_wd=(tok_scr_d,) if rt == 0 else ())
        tok_ln2, tok_x1 = emit_ln_pass(rt, [("sync", x0), ("scalar", fT_scr)],
                                       ("gpsimd", x1_scr, "x1w"), ln2T, rt * RT, RPC,
                                       extra_w=(tok_fT,))
        tok_ln2_parts.append(tok_ln2)
        tok_x1_parts.append(tok_x1)
        pump_prep(4)
    tok_ln2_all = latest(*tok_ln2_parts)
    tok_x1_all = latest(*tok_x1_parts)

    pump_prep(len(prep_steps))
    tok_prep = prep_state["last"]

    wst_free = [(), ()]
    ctx_toks = []
    cs_guard = ()
    vth_guard = ()
    mix_end_guard = ()
    last_vth = {"t": None}
    pq_evg = {0: None, 1: None}
    pchain.setdefault("pb2", None)
    pchain.setdefault("pb3", None)
    pchain.setdefault("pb4", None)
    pchain.setdefault("pb5", None)

    for rt in range(NRT):
        vdma_toks = []
        qk_ev_toks = []
        vstg_guard = ()
        for m_i, (mT, bT_t, kind) in enumerate(((wvT, bvT, "v"), (wqT, bqT, "q"),
                                                (wkT, bkT, "k"))):
            for occ in range(ICC):
                wsl = occ % 2
                lw = tuple(wst_free[wsl])
                if rt == 0 and m_i == 0 and occ < 2:
                    lw = lw + (tok_prep, tok_ln2_all)
                if kind == "q" and occ < 2:
                    lw = lw + tuple(mix_end_guard)
                tok_wl = plan.dma("sync", wst[wsl][:, :],
                                  mT.rearrange("(i p) o -> p i o", p=128)
                                  [:, :, occ * 128:(occ + 1) * 128],
                                  stream=f"wst{wsl}", waits=lw)
                bank = occ % 2
                tok_mm = None
                for icc in range(ICC):
                    wts = ()
                    if icc == 0:
                        wts = latest(tok_wl, tok_ln2_all, tok_bvT, pq_evg[bank])
                    tok_mm = plan.step("tensor",
                                       (lambda e, icc=icc, bank=bank, wsl=wsl, rt=rt:
                                        e.matmul(pb[bank][:, :],
                                                 wst[wsl][:, icc * 128:(icc + 1) * 128],
                                                 ln2T[:, :].rearrange("p (i r) -> p i r", r=RPC)
                                                 [:, icc, rt * RT:(rt + 1) * RT],
                                                 start=(icc == 0), stop=(icc == ICC - 1))),
                                       waits=wts, inc=sPE if icc == ICC - 1 else None)
                if kind == "q":
                    dstw = qTr[:, :].rearrange("p (o r) -> p o r", r=RT)[:, occ, :]
                elif kind == "k":
                    dstw = kTr[:, :].rearrange("p (o r) -> p o r", r=RT)[:, occ, :]
                else:
                    dstw = vstg[:, :]
                ev_w = (tok_mm,) + (vstg_guard if kind == "v" else ())
                if kind == "v" and occ == 0:
                    ev_w = ev_w + (tok_x1_all,)
                tok_ev = plan.step("vector",
                                   (lambda e, bank=bank, occ=occ, dstw=dstw, bT_t=bT_t:
                                    e.tensor_scalar(out=dstw, in0=pb[bank][:, :],
                                                    scalar1=bT_t[:, occ:occ + 1], scalar2=None,
                                                    op0=ALU.add)),
                                   waits=ev_w, inc=sDV)
                pq_evg[bank] = tok_ev
                if kind == "v":
                    tok_vd = plan.dma("gpsimd", vt_scr[occ * 128:(occ + 1) * 128, :],
                                      vstg[:, :], stream="vtw",
                                      waits=latest(tok_ev, *( (last_vth["t"],) if occ == 0 else () )))
                    vdma_toks.append(tok_vd)
                    vstg_guard = (tok_vd,)
                else:
                    qk_ev_toks.append(tok_ev)
                wst_free[wsl] = (tok_mm, tok_ev)
        tok_qk_all = latest(*qk_ev_toks)

        sc_war = {4: None, 5: None}
        vsb_war = {0: None, 1: None}
        cse_chain = {0: None, 1: None, 2: None, 3: None}
        prev_exp = None
        prev_pn = None
        prev_mm2 = None
        for half in range(2):
            tok_vh = plan.dma("sync",
                              vthalf[:, :].rearrange("p (o r) -> p o r", r=256),
                              vt_scr.rearrange("(o p) r -> p o r", p=128)
                              [:, :, half * 256:(half + 1) * 256],
                              stream="vth",
                              waits=latest(*vdma_toks, *vth_guard))
            for g in range(16):
                s0 = half * 256 + g * 16
                scb = 4 + g % 2
                vtb = 6 + g % 2
                tok_mm1 = None
                for i in range(16):
                    s_ = s0 + i
                    wts = ()
                    if i == 0:
                        wts = latest(tok_qk_all, sc_war[scb], pq_evg[0], pq_evg[1],
                                     pchain[f"pb{scb}"])
                    tok_mm1 = plan.step("tensor",
                                        (lambda e, i=i, s_=s_, scb=scb:
                                         e.matmul(pb[scb][0:32, i * 32:(i + 1) * 32],
                                                  qTr[:, :].rearrange("p (h s) -> p h s", s=RT)
                                                  [:, :, s_],
                                                  kTr[:, :].rearrange("p (h s) -> p h s", s=RT)
                                                  [:, :, s_],
                                                  start=(i == 0), stop=(i == 15))),
                                        waits=wts, inc=sPE if i == 15 else None)
                tok_vt_f = []
                tok_vsb0 = None
                for fill in range(2):
                    tk = None
                    for i8 in range(8):
                        i = fill * 8 + i8
                        s_ = s0 + i
                        wts = ()
                        if i == 0:
                            wts = latest(tok_vh, vsb_war[vtb - 6], pchain[f"pb{vtb}"])
                        elif i == 8:
                            wts = latest(tok_vsb0,)
                        tk = plan.step("tensor",
                                       (lambda e, i8=i8, s_=s_, vtb=vtb, half=half:
                                        e.matmul(pbB[vtb][0:32, i8 * 128:(i8 + 1) * 128],
                                                 vthalf[:, :].rearrange("p (h s) -> p h s", s=256)
                                                 [:, :, s_ - half * 256],
                                                 ident[:, :],
                                                 start=(i8 == 0), stop=(i8 == 7),
                                                 is_transpose=True)),
                                       waits=wts, inc=sPE if i8 == 7 else None)
                    tok_vt_f.append(tk)
                    if fill == 0:
                        tok_vsb0 = plan.step("vector",
                                             (lambda e, vtb=vtb:
                                              e.tensor_copy(out=vsb[0:32, 0:1024],
                                                            in_=pbB[vtb][0:32, 0:1024])),
                                             waits=latest(tk, prev_mm2), inc=sDV)
                tok_vsb1 = plan.step("vector",
                                     (lambda e, vtb=vtb:
                                      e.tensor_copy(out=vsb[0:32, 1024:2048],
                                                    in_=pbB[vtb][0:32, 0:1024])),
                                     waits=latest(tok_vt_f[1], prev_mm2), inc=sDV)
                vsb_war[vtb - 6] = tok_vsb1
                pchain[f"pb{vtb}"] = tok_vsb1
                tok_mx = plan.step("vector",
                                   (lambda e, scb=scb:
                                    e.reduce_max(out=mxs[0:32, :],
                                                 in_=pb[scb][0:32, :]
                                                 .rearrange("p (s t) -> p s t", t=32),
                                                 axis=AXX)),
                                   waits=(tok_mm1,), inc=sDV)
                tok_sub = plan.step("vector",
                                    (lambda e, scb=scb:
                                     e.tensor_tensor(out=sub_sb[0:32, :]
                                                     .rearrange("p (s t) -> p s t", t=32),
                                                     in0=pb[scb][0:32, :]
                                                     .rearrange("p (s t) -> p s t", t=32),
                                                     in1=mxs[0:32, :].unsqueeze(2)
                                                     .broadcast_to([32, 16, 32]),
                                                     op=ALU.subtract)),
                                    waits=latest(tok_mx, prev_exp), inc=sDV)
                sc_war[scb] = tok_sub
                pchain[f"pb{scb}"] = tok_sub
                tok_exp = plan.step("scalar",
                                    (lambda e: e.activation(out=prob[0:32, :],
                                                            in_=sub_sb[0:32, :], func=AF.Exp,
                                                            scale=float(SQD))),
                                    waits=latest(tok_sub, prev_pn), inc=sAC)
                prev_exp = tok_exp
                tok_sum = plan.step("vector",
                                    (lambda e: e.reduce_sum(out=dens[0:32, :],
                                                            in_=prob[0:32, :]
                                                            .rearrange("p (s t) -> p s t", t=32),
                                                            axis=AXX)),
                                    waits=(tok_exp,), inc=sDV)
                tok_rec = plan.step("vector",
                                    lambda e: e.reciprocal(out=recs[0:32, :], in_=dens[0:32, :]),
                                    waits=(tok_sum,), inc=sDV)
                tok_pn = plan.step("vector",
                                   (lambda e: e.tensor_tensor(out=pn[0:32, :]
                                                              .rearrange("p (s t) -> p s t", t=32),
                                                              in0=prob[0:32, :]
                                                              .rearrange("p (s t) -> p s t", t=32),
                                                              in1=recs[0:32, :].unsqueeze(2)
                                                              .broadcast_to([32, 16, 32]),
                                                              op=ALU.mult)),
                                   waits=(tok_rec,), inc=sDV)
                prev_pn = tok_pn
                tok_pt = plan.step("vector",
                                   lambda e: e.transpose(out=pt[0:32, :], in_=pn[0:32, :]),
                                   waits=latest(tok_pn, prev_mm2), inc=sDV)
                mm2_toks = []
                cse_now = []
                slotbase = (g % 2) * 16
                for p4 in range(4):
                    ctb = 2 + (p4 % 2)
                    tk = None
                    for i4 in range(4):
                        i = p4 * 4 + i4
                        wts = ()
                        if i4 == 0:
                            wts = latest(tok_pt, tok_vsb1, cse_chain[p4], cse_chain[p4 ^ 2])
                        tk = plan.step("tensor",
                                       (lambda e, i=i, i4=i4, ctb=ctb:
                                        e.matmul(pb[ctb][0:32, i4 * 128:(i4 + 1) * 128],
                                                 pt[0:32, i * 32:(i + 1) * 32],
                                                 vsb[0:32, i * 128:(i + 1) * 128],
                                                 start=(i4 == 0), stop=(i4 == 3))),
                                       waits=wts, inc=sPE if i4 == 3 else None)
                    mm2_toks.append(tk)
                    tok_cse = plan.step("scalar",
                                        (lambda e, p4=p4, slotbase=slotbase, ctb=ctb:
                                         e.copy(out=cs[0:32,
                                                       (slotbase + p4 * 4) * 128:
                                                       (slotbase + p4 * 4 + 4) * 128],
                                                in_=pb[ctb][0:32, :])),
                                        waits=latest(tk, *cs_guard), inc=sAC)
                    cse_chain[p4] = tok_cse
                    cse_now.append(tok_cse)
                prev_mm2 = latest(*mm2_toks)
                if g % 2 == 1:
                    rowbase = rt * RT + half * 256 + (g - 1) * 16
                    tok_cd = plan.dma("gpsimd",
                                      ctx_scr[rowbase:rowbase + 32, :]
                                      .rearrange("t (h d) -> h t d", d=128),
                                      cs[0:32, :].rearrange("p (t d) -> p t d", d=128),
                                      stream="cxw",
                                      waits=latest(*cse_now, cse_chain[0], cse_chain[1],
                                                   cse_chain[2], cse_chain[3]))
                    cs_guard = (tok_cd,)
                    ctx_toks.append(tok_cd)
            vth_guard = (tok_vt_f[1],)
        mix_end_guard = latest(sc_war[4], sc_war[5], cse_chain[0], cse_chain[1],
                               cse_chain[2], cse_chain[3], prev_mm2)
        last_vth["t"] = tok_vh
    pchain["pb2"] = latest(cse_chain[0], cse_chain[2])
    pchain["pb3"] = latest(cse_chain[1], cse_chain[3])
    tok_ctx_all = latest(*ctx_toks)

    out_toks = []
    tok_x2_all = None
    for rt in range(NRT):
        tok_T, tok_x2 = emit_ln_pass(rt, [("sync", x1_scr), ("scalar", ctx_scr)],
                                     ("gpsimd", x2_scr, "x2w"), lnT2, 0, RT,
                                     extra_w=(tok_x1_all, tok_ctx_all))
        tok_x2_all = tok_x2
        tok_n = plan.step("vector",
                          (lambda e: e.tensor_scalar(out=lnTn2[:, :], in0=lnT2[:, :],
                                                     scalar1=-1.0, scalar2=None, op0=ALU.mult)),
                          waits=(tok_T,), inc=sDV)
        tok_fT = emit_ffn(tok_T, tok_n, (), dreg_w=(tok_x2,),
                          lnT_t=lnT2, lnTn_t=lnTn2, dsln_t=dsln2)
        for sub in range(SUBRT):
            s = rt * SUBRT + sub
            sl = 0
            t1 = plan.dma("sync", xsub[sl][:, :], x2_scr[s * 128:(s + 1) * 128, :],
                          stream="xs", waits=tuple(xguard[sl]) + ((tok_x2, tok_fT) if sub == 0 else ()))
            xguard[sl] = ()
            t2 = plan.dma("scalar", ftsub[:, :], fT_scr[sub * 128:(sub + 1) * 128, :],
                          stream="fts", waits=tuple(ftguard["t"]) + ((tok_fT,) if sub == 0 else ()))
            t3 = plan.step("vector",
                           (lambda e, sl=sl: e.tensor_tensor(out=xsub[sl][:, :], in0=xsub[sl][:, :],
                                                             in1=ftsub[:, :], op=ALU.add)),
                           waits=(t1, t2), inc=sDV)
            ftguard["t"] = (t3,)
            t4 = plan.dma("gpsimd", out[s * 128:(s + 1) * 128, :], xsub[sl][:, :],
                          sem=sOUT, waits=(t3,))
            xguard[sl] = (t3, t4)
            out_toks.append(t4)

    if debug_taps:
        taps = [
            ("d_scr_g", scr_g), ("d_scr_u", scr_u), ("d_scr_d", scr_d),
            ("d_h", h_scr), ("d_fT", fT_scr),
            ("d_x1", x1_scr), ("d_x2", x2_scr), ("d_ctx", ctx_scr),
            ("d_vt", vt_scr),
        ]
        wall = (sOUT, sOUT.count)
        for nm, src in taps:
            dt = nc.dram_tensor(nm, list(src.shape), src.dtype, kind="ExternalOutput").ap()
            plan.dma("gpsimd", dt, src, sem=sOUT, waits=(wall,))
        for nm, t, w in (("d_scales", scales, 8), ("d_sval", sval, 16)):
            dt = nc.dram_tensor(nm, [t.shape[0], w], F32, kind="ExternalOutput").ap()
            plan.dma("gpsimd", dt, t[0:t.shape[0], 0:w], sem=sOUT, waits=(wall,))

    plan.step("gpsimd", None, waits=((sOUT, sOUT.count),))

    plan.realize()
    return nc


_cache = {}


def kernel(L0_attn_hidden_states, L0_attn_wq, L0_attn_bq, L0_attn_wk, L0_attn_bk,
           L0_attn_wv, L0_attn_bv, L0_attn_wl, L0_attn_bl,
           L0_ffn_w_gate, L0_ffn_w_up, L0_ffn_w_down):
    x = np.ascontiguousarray(np.asarray(L0_attn_hidden_states).reshape(8192, 4096))
    args = {
        "wq": np.ascontiguousarray(np.asarray(L0_attn_wq)),
        "wk": np.ascontiguousarray(np.asarray(L0_attn_wk)),
        "wv": np.ascontiguousarray(np.asarray(L0_attn_wv)),
        "bq": np.ascontiguousarray(np.asarray(L0_attn_bq)),
        "bk": np.ascontiguousarray(np.asarray(L0_attn_bk)),
        "bv": np.ascontiguousarray(np.asarray(L0_attn_bv)),
        "wg": np.ascontiguousarray(np.asarray(L0_ffn_w_gate)),
        "wu": np.ascontiguousarray(np.asarray(L0_ffn_w_up)),
        "wd": np.ascontiguousarray(np.asarray(L0_ffn_w_down)),
    }
    if "nc" not in _cache:
        _cache["nc"] = build_program()
    nc = _cache["nc"]
    in_maps = [dict(args, x0=x[c * RPC:(c + 1) * RPC]) for c in range(NCORES)]
    res = run_bass_kernel_spmd(nc, in_maps, core_ids=list(range(NCORES)))
    outs = [res.results[c]["out"] for c in range(NCORES)]
    return np.concatenate(outs, axis=0).reshape(4, 2048, 4096)


if __name__ == "__main__":
    import time
    t0 = time.time()
    nc = build_program()
    print("built ok in %.1fs; instructions: %d" % (time.time() - t0, nc.n_instructions()))
